# version 27
# baseline (speedup 1.0000x reference)
import math
import numpy as np
import ml_dtypes
from contextlib import ExitStack
import concourse.bass as bass
import concourse.mybir as mybir
from concourse.bass_utils import run_bass_kernel_spmd

F32 = mybir.dt.float32
BF16 = mybir.dt.bfloat16
FP8 = mybir.dt.float8e4
AF = mybir.ActivationFunctionType
ALU = mybir.AluOpType
AXX = mybir.AxisListType.X

S_LEN = 2048
D = 1024
NT = 16
ND = 8
MASK_NEG = -1.0e4


def build_nc(n_win=NT, do_phase2=True, last_parts=3, skip=()):
    nc = bass.Bass("TRN2", target_bir_lowering=False)

    x_d = nc.dram_tensor("x", [S_LEN, D], F32, kind="ExternalInput").ap()
    w_d = nc.dram_tensor("w", [2 * D, D], F32, kind="ExternalInput").ap()
    wf_d = nc.dram_tensor("wf", [D, D], F32, kind="ExternalInput").ap()
    bias_d = nc.dram_tensor("bias", [1, D], F32, kind="ExternalInput").ap()
    scl_d = nc.dram_tensor("scl", [1, 2], F32, kind="ExternalInput").ap()
    mask_d = nc.dram_tensor("mask", [128, 128], F32, kind="ExternalInput").ap()
    idn_d = nc.dram_tensor("idn", [128, 128], BF16, kind="ExternalInput").ap()
    y_d = nc.dram_tensor("y", [S_LEN, D], F32, kind="ExternalOutput").ap()
    wvbf_d = nc.dram_tensor("wvbf_scratch", [D, D], BF16).ap()
    wfbf_d = nc.dram_tensor("wfbf_scratch", [D, D], BF16).ap()

    sched = []

    def op(eng, fn):
        sched.append((eng, fn))

    cnt = {"pe": 0, "ve": 0, "ac": 0, "ds": 0, "vs": 0, "vt": 0, "vp": 0, "vf": 0,
           "xs0": 0, "xs1": 0, "xs2": 0, "xs3": 0, "wks": 0, "wvs": 0, "wfs": 0,
           "xr0": 0, "xr1": 0, "xr2": 0, "xr3": 0, "do0": 0, "do1": 0, "po": 0,
           "xfs": 0, "wts": 0, "fts": 0, "qs0": 0}

    ctx = ExitStack()
    with ctx:
        sem = {k: ctx.enter_context(nc.semaphore(k)) for k in cnt}

        def wait(eng, key, val):
            if val <= 0:
                return
            op(eng, lambda e, s=sem[key], v=val: e.wait_ge(s, v))

        def bump(key, k=1):
            cnt[key] += k
            return cnt[key]

        xfm = ctx.enter_context(nc.sbuf_tensor("xfm", [128, ND, S_LEN], BF16))
        ct = ctx.enter_context(nc.sbuf_tensor("ct", [128, ND, S_LEN], BF16))
        vtm = ctx.enter_context(nc.sbuf_tensor("vtm", [128, NT, D], BF16))
        wft = ctx.enter_context(nc.sbuf_tensor("wft", [128, ND, D], BF16))
        biasb = ctx.enter_context(nc.sbuf_tensor("biasb", [128, D], F32))
        maskd = ctx.enter_context(nc.sbuf_tensor("maskd", [128, 128], F32))
        idn = ctx.enter_context(nc.sbuf_tensor("idn_sb", [128, 128], BF16))
        ones = ctx.enter_context(nc.sbuf_tensor("ones", [1, 128], F32))
        bias_sb = ctx.enter_context(nc.sbuf_tensor("bias_sb", [1, D], F32))
        scl_sb = ctx.enter_context(nc.sbuf_tensor("scl_sb", [1, 2], F32))
        sclb = ctx.enter_context(nc.sbuf_tensor("sclb", [128, 2], F32))
        invh = ctx.enter_context(nc.sbuf_tensor("invh", [128, 1], F32))
        ssb = ctx.enter_context(nc.sbuf_tensor("ssb", [128, 1], F32))
        cst = ctx.enter_context(nc.sbuf_tensor("cst", [128, 3], F32))
        stats = ctx.enter_context(nc.sbuf_tensor("stats", [128, 30], F32))
        pmax = ctx.enter_context(nc.sbuf_tensor("pmax", [128, 8], F32))

        def stat(k, par):
            return stats[:, 2 * k + par: 2 * k + par + 1]
        NMAX, LSUM, RECI, ST1, NM1, VS1, U1, RS1, ST2, NM2, VS2, RS2, ST1B, ST2B = range(14)
        LSUMB = 13
        LSUMB = 14

        op("sync", lambda e: e.dma_start(out=bias_sb[:], in_=bias_d).then_inc(sem["ds"], 16))
        op("sync", lambda e: e.dma_start(out=scl_sb[:], in_=scl_d).then_inc(sem["ds"], 16))
        op("sync", lambda e: e.dma_start(out=maskd[:], in_=mask_d).then_inc(sem["ds"], 16))
        op("sync", lambda e: e.dma_start(out=idn[:], in_=idn_d).then_inc(sem["ds"], 16))
        ds_consts = 64

        op("vector", lambda e: e.memset(cst[:, 0:1], 1e-5))
        op("vector", lambda e: e.memset(cst[:, 1:2], 1.0 / 1024.0))
        op("vector", lambda e: e.memset(cst[:, 2:3], -0.5))
        op("vector", lambda e: e.memset(ones[:, :], 1.0).then_inc(sem["ve"], 1))
        ve_init = bump("ve")

        def _preload(e):
            try:
                from concourse.hw_specs import get_activation_tables
                need = {AF.Exp, AF.Ln, AF.Copy, AF.Square}
                for idx, funcs in enumerate(get_activation_tables(e.bass.m.arch).values()):
                    if need.issubset(funcs):
                        e.add_instruction(mybir.InstLoadActFuncSet(
                            name=e.bass.get_next_instruction_name(),
                            act_func_set_id=idx, ins=[], outs=[]))
                        return
            except Exception:
                pass
        op("scalar", _preload)

        ctx1 = ExitStack()
        with ctx1:
            wbufA = ctx1.enter_context(nc.sbuf_tensor("wbufA", [128, ND, D], BF16))
            wvt = ctx1.enter_context(nc.sbuf_tensor("wvt", [128, ND, D], BF16))
            xtm = ctx1.enter_context(nc.sbuf_tensor("xtm", [128, NT, D], BF16))
            trp = [ctx1.enter_context(nc.psum_tensor(f"trp{p}", [128, ND, 128], BF16)) for p in range(2)]
            cps = [ctx1.enter_context(nc.psum_tensor(f"cps{c}", [128, 512], F32)) for c in range(4)]

            xchunks = [(0, 1), (1, 4), (4, 10), (10, 16)]
            for c, (t0, t1) in enumerate(xchunks):
                def f(e, c=c, t0=t0, t1=t1):
                    e.dma_start(out=xtm[:, t0:t1, :],
                                in_=x_d[t0 * 128:t1 * 128, :].rearrange("(t p) d -> p t d", p=128)
                                ).then_inc(sem[f"xs{c}"], 16)
                op("gpsimd", f)
                bump(f"xs{c}", 16)
            op("gpsimd", lambda e: e.dma_start(out=wbufA[:, :, :],
                                               in_=w_d[0:D, :].rearrange("(t p) d -> p t d", p=128))
               .then_inc(sem["wks"], 16))
            bump("wks", 16)
            op("gpsimd", lambda e: e.dma_start(out=wvbf_d[:, :], in_=w_d[D:2 * D, :])
               .then_inc(sem["wvs"], 16))
            bump("wvs", 16)
            op("gpsimd", lambda e: e.dma_start(out=wfbf_d[:, :], in_=wf_d)
               .then_inc(sem["wfs"], 16))
            bump("wfs", 16)

            ve_xt = [0] * NT
            pe_xt = [0] * NT
            xwait = {t0: c for c, (t0, t1) in enumerate(xchunks)}
            for t in range(NT):
                if t in xwait:
                    wait("tensor", f"xs{xwait[t]}", 16)
                if t == 0:
                    wait("tensor", "ds", ds_consts)
                if t >= 2:
                    wait("tensor", "ve", ve_xt[t - 2])
                for d in range(ND):
                    def f(e, t=t, d=d, last=(d == ND - 1)):
                        ins = e.transpose(trp[t % 2][:, d, :], xtm[:, t, d * 128:(d + 1) * 128], idn[:])
                        if last:
                            ins.then_inc(sem["pe"], 1)
                    op("tensor", f)
                pe_xt[t] = bump("pe")
                wait("vector", "pe", pe_xt[t])
                op("vector", lambda e, t=t: e.tensor_copy(xfm[:, :, t * 128:(t + 1) * 128], trp[t % 2][:, :, :])
                   .then_inc(sem["ve"], 1))
                ve_xt[t] = bump("ve")

            wait("sync", "wvs", 16)
            for dt in range(ND):
                op("sync", lambda e, dt=dt: e.dma_start_transpose(
                    wvt[:, dt, :], wvbf_d[:, dt * 128:(dt + 1) * 128]).then_inc(sem["wts"], 16))
                bump("wts", 16)
            wait("sync", "wfs", 16)
            for it in range(ND):
                op("sync", lambda e, it=it: e.dma_start_transpose(
                    wft[:, it, :], wfbf_d[:, it * 128:(it + 1) * 128]).then_inc(sem["fts"], 16))
                bump("fts", 16)

            wait("tensor", "ds", ds_consts)
            wait("tensor", "ve", ve_init)
            op("tensor", lambda e: e.matmul(cps[0][:, 0:2], ones[:, :], scl_sb[:, :], start=True, stop=True))
            op("tensor", lambda e: e.matmul(cps[1][:, 0:512], ones[:, :], bias_sb[:, 0:512], start=True, stop=True))
            op("tensor", lambda e: e.matmul(cps[2][:, 0:512], ones[:, :], bias_sb[:, 512:1024], start=True, stop=True)
               .then_inc(sem["pe"], 1))
            pe_bcast = bump("pe")
            wait("vector", "pe", pe_bcast)
            op("vector", lambda e: e.tensor_copy(sclb[:, :], cps[0][:, 0:2]))
            op("vector", lambda e: e.drain())
            op("vector", lambda e: e.reciprocal(invh[:, :], sclb[:, 0:1]))
            op("vector", lambda e: e.drain())
            op("vector", lambda e: e.tensor_scalar_mul(invh[:, :], invh[:, :], 0.5))
            op("vector", lambda e: e.tensor_mul(ssb[:, :], sclb[:, 0:1], sclb[:, 1:2]))
            op("vector", lambda e: e.tensor_copy(biasb[:, 0:512], cps[1][:, 0:512]))
            op("vector", lambda e: e.tensor_copy(biasb[:, 512:1024], cps[2][:, 0:512])
               .then_inc(sem["ve"], 1))
            ve_bcast = bump("ve")

            wait("tensor", "ve", ve_xt[NT - 1])
            wait("tensor", "wks", 16)
            wait("tensor", "ve", ve_bcast)
            ve_ct_half = {}
            kglob = 0
            for dt in range(ND):
                for h in range(2):
                    pair = kglob % 2
                    if kglob >= 2:
                        wait("tensor", "ve", ve_ct_half[kglob - 2])
                    for j in range(ND):
                        for cc in range(2):
                            c = h * 2 + cc
                            bank = cps[2 * pair + cc]
                            def f(e, dt=dt, j=j, c=c, bank=bank, last=(j == ND - 1 and cc == 1)):
                                ins = e.matmul(bank[:, 0:512],
                                               wbufA[:, j, dt * 128:(dt + 1) * 128],
                                               xfm[:, j, c * 512:(c + 1) * 512],
                                               start=(j == 0), stop=(j == ND - 1))
                                if last:
                                    ins.then_inc(sem["pe"], 1)
                            op("tensor", f)
                    pe_h = bump("pe")
                    wait("vector", "pe", pe_h)
                    for cc in range(2):
                        c = h * 2 + cc
                        bank = cps[2 * pair + cc]
                        def f(e, dt=dt, c=c, bank=bank, last=(cc == 1)):
                            ins = e.tensor_copy(ct[:, dt, c * 512:(c + 1) * 512], bank[:, 0:512])
                            if last:
                                ins.then_inc(sem["ve"], 1)
                        op("vector", f)
                    ve_ct_half[kglob] = bump("ve")
                    kglob += 1
            pe_ct_done = cnt["pe"]
            ve_ct_done = cnt["ve"]

            wait("tensor", "wts", 128)
            ve_v = [0] * NT
            for st in range(NT):
                pair = st % 2
                if st >= 2:
                    wait("tensor", "ve", ve_v[st - 2])
                elif st == 0:
                    wait("tensor", "ve", ve_ct_done)
                for d in range(ND):
                    for cc in range(2):
                        def f(e, st=st, d=d, cc=cc, pair=pair, last=(d == ND - 1 and cc == 1)):
                            ins = e.matmul(cps[2 * pair + cc][:, 0:512],
                                           xfm[:, d, st * 128:(st + 1) * 128],
                                           wvt[:, d, cc * 512:(cc + 1) * 512],
                                           start=(d == 0), stop=(d == ND - 1))
                            if last:
                                ins.then_inc(sem["pe"], 1)
                        op("tensor", f)
                pe_v = bump("pe")
                wait("vector", "pe", pe_v)
                for cc in range(2):
                    def f(e, st=st, cc=cc, pair=pair, last=(cc == 1)):
                        ins = e.tensor_copy(vtm[:, st, cc * 512:(cc + 1) * 512], cps[2 * pair + cc][:, 0:512])
                        if last:
                            ins.then_inc(sem["ve"], 1)
                    op("vector", f)
                ve_v[st] = bump("ve")

            wait("tensor", "fts", 128)
            ve_phase1_done = cnt["ve"]
            pe_phase1_done = cnt["pe"]

        ctx2 = ExitStack()
        with ctx2:
            sps = [ctx2.enter_context(nc.psum_tensor(f"sps{c}", [128, 512], F32)) for c in range(4)]
            pvps = [ctx2.enter_context(nc.psum_tensor(f"pvps{c}", [128, 512], F32)) for c in range(2)]
            trp2 = [ctx2.enter_context(nc.psum_tensor(f"trp2_{p}", [128, 8, 128], BF16)) for p in range(2)]

            s_sb = [ctx2.enter_context(nc.sbuf_tensor(f"s_sb{p}", [128, S_LEN], F32)) for p in range(2)]
            p_bf = [ctx2.enter_context(nc.sbuf_tensor(f"p_bf{p}", [128, S_LEN], BF16)) for p in range(2)]
            pt = ctx2.enter_context(nc.sbuf_tensor("pt", [128, NT, 128], BF16))
            xr = [ctx2.enter_context(nc.sbuf_tensor(f"xr{p}", [128, D], F32)) for p in range(2)]
            t_sb = [ctx2.enter_context(nc.sbuf_tensor(f"t_sb{p}", [128, D], F32)) for p in range(2)]
            y1 = [ctx2.enter_context(nc.sbuf_tensor(f"y1_{p}", [128, D], F32)) for p in range(2)]
            sq = [ctx2.enter_context(nc.sbuf_tensor(f"sq{p}", [128, D], F32)) for p in range(2)]
            q8 = [ctx2.enter_context(nc.sbuf_tensor(f"q8_{p}", [128, D], FP8)) for p in range(2)]
            q8b = [ctx2.enter_context(nc.sbuf_tensor(f"q8b_{p}", [128, D], FP8)) for p in range(2)]
            yqb = [ctx2.enter_context(nc.sbuf_tensor(f"yqb{p}", [128, D], BF16)) for p in range(2)]
            yqt = ctx2.enter_context(nc.sbuf_tensor("yqt", [128, ND, 128], BF16))
            t2 = [ctx2.enter_context(nc.sbuf_tensor(f"t2_{p}", [128, D], F32)) for p in range(2)]
            y2 = [ctx2.enter_context(nc.sbuf_tensor(f"y2_{p}", [128, D], F32)) for p in range(2)]



            dg_xr = [0] * NT
            ve_pvevac = [0] * NT
            ac_scopy = [0] * NT
            ve_diag = [0] * NT
            ve_max = [0] * NT
            ac_exp = [0] * NT
            ve_tevac = [0] * NT
            pe_s = [0] * NT
            pe_pv = [0] * NT
            pe_yt = [0] * NT
            pe_ffn = [0] * NT
            ve_sevac_done = [0] * NT
            ve_ytevac = [0] * NT
            ve_ffnevac = [0] * NT
            ve_y2 = [0] * NT
            ve_q = [0] * NT
            ac_rstd1 = [0] * NT
            ac_rstd2 = [0] * NT
            do_store = [0] * NT
            ve_nm1 = [0] * NT
            ve_pvstt_c = [[0, 0] for _ in range(NT)]
            pe_ffn_c = [[0, 0] for _ in range(NT)]
            ve_scopyd = [0] * NT
            tev_groups = [None] * NT
            pe_s_chunk = [None] * NT
            ac_scopy_c = [[0] * 4 for _ in range(NT)]
            pe_yt_a = [0] * NT
            ve_nm2 = [0] * NT
            ac_exp_a = [0] * NT
            exp_split = [False] * NT
            ve_ytev_a = [0] * NT
            ve_y1 = [0] * NT
            yt_tb = [0] * NT
            trp2_free2 = [("ac", 0), ("ac", 0)]
            tb_ctr = 0

            def emit_xr(i):
                nonlocal dg_xr
                k = f"xr{i % 4}"
                if i >= 2:
                    wait("gpsimd", "vp", ve_pvevac[i - 2])
                op("gpsimd", lambda e, i=i, k=k: e.dma_start(out=xr[i % 2][:, :], in_=x_d[i * 128:(i + 1) * 128, :])
                   .then_inc(sem[k], 16))
                v1 = bump(k, 16)
                wait("gpsimd", k, v1)
                op("gpsimd", lambda e, i=i, k=k: e.dma_start(out=xr[i % 2][:, :], in_=x_d[i * 128:(i + 1) * 128, :],
                                                             accum_op=ALU.add).then_inc(sem[k], 16))
                dg_xr[i] = bump(k, 16)

            def emit_S(i):
                par = i % 2
                nch = (i + 1) * 128 // 512 + (1 if ((i + 1) * 128) % 512 else 0)
                if i == 0:
                    wait("tensor", "ve", ve_phase1_done)
                else:
                    wait("tensor", "ac", ac_scopy[i - 1])
                pe_s_chunk[i] = [0] * nch
                for c in range(nch):
                    w = min(512, (i + 1) * 128 - c * 512)
                    for j in range(ND):
                        def f(e, i=i, j=j, c=c, w=w, last=(j == ND - 1)):
                            ins = e.matmul(sps[c][:, 0:w],
                                           ct[:, j, i * 128:(i + 1) * 128],
                                           xfm[:, j, c * 512:c * 512 + w],
                                           start=(j == 0), stop=(j == ND - 1))
                            if last:
                                ins.then_inc(sem["pe"], 1)
                        op("tensor", f)
                    pe_s_chunk[i][c] = bump("pe")
                pe_s[i] = pe_s_chunk[i][-1]

            def emit_scopy(i):
                par = i % 2
                W = (i + 1) * 128
                nch = W // 512 + (1 if W % 512 else 0)
                if i >= 2:
                    wait("scalar", "vs", ve_max[i - 2])
                for c in range(nch):
                    w = min(512, W - c * 512)
                    wait("scalar", "pe", pe_s_chunk[i][c])
                    def f(e, c=c, w=w, par=par):
                        e.copy(s_sb[par][:, c * 512:c * 512 + w], sps[c][:, 0:w]).then_inc(sem["ac"], 1)
                    op("scalar", f)
                    ac_scopy_c[i][c] = bump("ac")
                ac_scopy[i] = ac_scopy_c[i][nch - 1]
                ve_scopyd[i] = 0

            def emit_diag(i):
                par = i % 2
                W = (i + 1) * 128
                nch = W // 512 + (1 if W % 512 else 0)
                dcol = i * 128
                if i >= 2:
                    wait("vector", "ac", ac_exp[i - 2])
                for c in range(nch - 1):
                    wait("vector", "ac", ac_scopy_c[i][c])
                    def f(e, c=c, par=par):
                        e.reduce_max(pmax[:, c:c + 1], s_sb[par][:, c * 512:(c + 1) * 512],
                                     axis=AXX, negate=True)
                    op("vector", f)
                wait("vector", "ac", ac_scopy_c[i][nch - 1])
                op("vector", lambda e: e.drain())
                op("vector", lambda e, par=par, dcol=dcol: e.tensor_add(
                    s_sb[par][:, dcol:dcol + 128], s_sb[par][:, dcol:dcol + 128], maskd[:])
                    .then_inc(sem["vs"], 1))
                ve_diag[i] = bump("vs")

            def emit_max(i):
                par = i % 2
                W = (i + 1) * 128
                nch = W // 512 + (1 if W % 512 else 0)
                w = W - (nch - 1) * 512
                op("vector", lambda e: e.drain())
                def f(e, c=nch - 1, w=w, par=par):
                    e.reduce_max(pmax[:, c:c + 1], s_sb[par][:, c * 512:c * 512 + w],
                                 axis=AXX, negate=True)
                op("vector", f)
                op("vector", lambda e: e.drain())
                op("vector", lambda e, par=par, nch=nch: e.tensor_reduce(
                    stat(NMAX, par), pmax[:, 0:nch], AXX, ALU.min).then_inc(sem["vs"], 1))
                ve_max[i] = bump("vs")
            def emit_exp(i):
                par = i % 2
                W = (i + 1) * 128
                Wa = min(W, 1024)
                wait("scalar", "vs", ve_max[i])
                if i >= 2:
                    wait("scalar", "pe", pe_t_last[i - 2])
                op("scalar", lambda e, par=par, Wa=Wa: e.activation(
                    p_bf[par][:, 0:Wa], s_sb[par][:, 0:Wa], AF.Exp,
                    bias=stat(NMAX, par), accum_out=stat(LSUM, par)).then_inc(sem["ac"], 1))
                ac_exp_a[i] = bump("ac")
                if W > 1024:
                    op("scalar", lambda e: e.drain())
                    op("scalar", lambda e, par=par, W=W: e.activation(
                        p_bf[par][:, 1024:W], s_sb[par][:, 1024:W], AF.Exp,
                        bias=stat(NMAX, par), accum_out=stat(LSUMB, par)).then_inc(sem["ac"], 1))
                    ac_exp[i] = bump("ac")
                else:
                    ac_exp[i] = ac_exp_a[i]
                exp_split[i] = (W > 1024)

            def emit_recip(i):
                par = i % 2
                wait("vector", "ac", ac_exp[i])
                if exp_split[i]:
                    op("vector", lambda e, par=par: e.tensor_add(stat(LSUM, par), stat(LSUM, par),
                                                                 stat(LSUMB, par)))
                    op("vector", lambda e: e.drain())
                op("vector", lambda e, par=par: e.reciprocal(stat(RECI, par), stat(LSUM, par))
                   .then_inc(sem["vs"], 1))
                op("vector", lambda e: e.drain())
                bump("vs")

            pe_t_last = [0] * NT

            def emit_T(i):
                nonlocal tb_ctr
                ntile = i + 1
                par = i % 2
                col = 0
                evs = []
                while col < ntile:
                    ng = min(4, ntile - col)
                    tb = tb_ctr % 2
                    tb_ctr += 1
                    wait("tensor", trp2_free2[tb][0], trp2_free2[tb][1])
                    if col < 8:
                        wait("tensor", "ac", ac_exp_a[i])
                    else:
                        wait("tensor", "ac", ac_exp[i])
                    for u in range(ng):
                        def f(e, par=par, col=col, u=u, tb=tb, last=(u == ng - 1)):
                            ins = e.transpose(trp2[tb][:, u, :],
                                              p_bf[par][:, (col + u) * 128:(col + u + 1) * 128], idn[:])
                            if last:
                                ins.then_inc(sem["pe"], 1)
                        op("tensor", f)
                    pe_g = bump("pe")
                    wait("scalar", "pe", pe_g)
                    op("scalar", lambda e, col=col, ng=ng, tb=tb: e.copy(
                        pt[:, col:col + ng, :], trp2[tb][:, 0:ng, :]).then_inc(sem["ac"], 1))
                    trp2_free2[tb] = ("ac", bump("ac"))
                    evs.append(trp2_free2[tb][1])
                    col += ng
                pe_t_last[i] = pe_g
                tev_groups[i] = list(evs)
                ve_tevac[i] = evs[-1]

            pv_free_ve = [("vp", 0)]

            def emit_PV(i):
                par = i % 2
                wait("tensor", pv_free_ve[0][0], pv_free_ve[0][1])
                for t in range(i + 1):
                    if t % 4 == 0:
                        wait("tensor", "ac", tev_groups[i][t // 4])
                    for cc in range(2):
                        def f(e, t=t, cc=cc, i=i, last=(t == i and cc == 1)):
                            ins = e.matmul(pvps[cc][:, 0:512], pt[:, t, :],
                                           vtm[:, t, cc * 512:(cc + 1) * 512],
                                           start=(t == 0), stop=(t == i))
                            if last:
                                ins.then_inc(sem["pe"], 1)
                        op("tensor", f)
                pe_pv[i] = bump("pe")

            def emit_pvstt(i):
                par = i % 2
                wait("vector", "pe", pe_pv[i])
                wait("vector", f"xr{i % 4}", dg_xr[i])
                for cc in range(2):
                    def f(e, cc=cc, par=par):
                        e.scalar_tensor_tensor(t_sb[par][:, cc * 512:(cc + 1) * 512],
                                               pvps[cc][:, 0:512], stat(RECI, par),
                                               xr[par][:, cc * 512:(cc + 1) * 512],
                                               ALU.mult, ALU.add,
                                               accum_out=stat(ST1 if cc == 0 else ST1B, par)
                                               ).then_inc(sem["vp"], 1)
                    op("vector", f)
                    ve_pvstt_c[i][cc] = bump("vp")
                ve_pvevac[i] = ve_pvstt_c[i][1]
                pv_free_ve[0] = ("vp", ve_pvevac[i])
                op("vector", lambda e: e.drain())
                op("vector", lambda e, par=par: e.tensor_scalar(stat(NM1, par), stat(ST1, par),
                                                                stat(ST1B, par), -1.0 / 1024.0,
                                                                ALU.add, ALU.mult).then_inc(sem["vp"], 1))
                ve_nm1[i] = bump("vp")

            def emit_ln1(i):
                par = i % 2
                wait("scalar", "vp", ve_nm1[i])
                op("scalar", lambda e, par=par: e.activation(sq[par][:, :], t_sb[par][:, :], AF.Square,
                                                             bias=stat(NM1, par), accum_out=stat(VS1, par)))
                op("scalar", lambda e: e.drain())
                op("scalar", lambda e, par=par: e.activation(stat(U1, par), stat(VS1, par), AF.Ln,
                                                             scale=cst[:, 1:2], bias=cst[:, 0:1]))
                op("scalar", lambda e: e.drain())
                op("scalar", lambda e, par=par: e.activation(stat(RS1, par), stat(U1, par), AF.Exp,
                                                             scale=cst[:, 2:3]).then_inc(sem["ac"], 1))
                ac_rstd1[i] = bump("ac")

            def emit_y1q(i):
                par = i % 2
                wait("vector", "ac", ac_rstd1[i])
                op("vector", lambda e, par=par: e.tensor_scalar(y1[par][:, :], t_sb[par][:, :],
                                                                stat(NM1, par), stat(RS1, par),
                                                                ALU.add, ALU.mult).then_inc(sem["vp"], 1))
                ve_y1[i] = bump("vp")
                op("vector", lambda e: e.drain())
                op("vector", lambda e, par=par: e.tensor_scalar(sq[par][:, :], y1[par][:, :],
                                                                invh[:, 0:1], 224.0, ALU.mult, ALU.min))
                op("vector", lambda e: e.drain())
                op("vector", lambda e, par=par: e.tensor_scalar(q8[par][:, :], sq[par][:, :],
                                                                -224.0, None, ALU.max).then_inc(sem["vp"], 1))
                vq8 = bump("vp")
                wait("sync", "vp", vq8)
                op("sync", lambda e, par=par: e.dma_start(
                    out=q8b[par][:, :].bitcast(mybir.dt.int8),
                    in_=q8[par][:, :].bitcast(mybir.dt.int8)).then_inc(sem["qs0"], 16))
                vqc = bump("qs0", 16)
                wait("vector", "qs0", vqc)
                op("vector", lambda e, par=par: e.tensor_scalar(yqb[par][:, :], q8b[par][:, :],
                                                                2.0, None, ALU.mult).then_inc(sem["vp"], 1))
                ve_q[i] = bump("vp")

            def emit_yT(i):
                nonlocal tb_ctr
                par = i % 2
                tb1 = tb_ctr % 2
                tb_ctr += 1
                tb2 = tb_ctr % 2
                tb_ctr += 1
                yt_tb[i] = (tb1, tb2)
                wait("tensor", "vp", ve_q[i])
                wait("tensor", trp2_free2[tb1][0], trp2_free2[tb1][1])
                for u in range(4):
                    def f(e, par=par, u=u, tb=tb1, last=(u == 3)):
                        ins = e.transpose(trp2[tb][:, u, :], yqb[par][:, u * 128:(u + 1) * 128], idn[:])
                        if last:
                            ins.then_inc(sem["pe"], 1)
                    op("tensor", f)
                pe_yt_a[i] = bump("pe")
                wait("tensor", trp2_free2[tb2][0], trp2_free2[tb2][1])
                for u in range(4, 8):
                    def f(e, par=par, u=u, tb=tb2, last=(u == 7)):
                        ins = e.transpose(trp2[tb][:, u - 4, :], yqb[par][:, u * 128:(u + 1) * 128], idn[:])
                        if last:
                            ins.then_inc(sem["pe"], 1)
                    op("tensor", f)
                pe_yt[i] = bump("pe")

            def emit_ytevac(i):
                tb1, tb2 = yt_tb[i]
                wait("scalar", "pe", pe_yt_a[i])
                op("scalar", lambda e, tb=tb1: e.copy(yqt[:, 0:4, :], trp2[tb][:, 0:4, :])
                   .then_inc(sem["ac"], 1))
                trp2_free2[tb1] = ("ac", bump("ac"))
                ve_ytev_a[i] = trp2_free2[tb1][1]
                wait("scalar", "pe", pe_yt[i])
                op("scalar", lambda e, tb=tb2: e.copy(yqt[:, 4:8, :], trp2[tb][:, 0:4, :])
                   .then_inc(sem["ac"], 1))
                trp2_free2[tb2] = ("ac", bump("ac"))
                ve_ytevac[i] = trp2_free2[tb2][1]

            def emit_FFN(i):
                wait("tensor", "ac", ve_ytev_a[i])
                pe_ffn_c[i] = [0, 0]
                for cc in range(2):
                    wait("tensor", "vp", ve_pvstt_c[min(i + 1, NT - 1)][cc])
                    if cc == 0:
                        pass
                    else:
                        wait("tensor", "ac", ve_ytevac[i])
                    for ic in range(ND):
                        if cc == 0 and ic == 4:
                            wait("tensor", "ac", ve_ytevac[i])
                        def f(e, ic=ic, cc=cc, last=(ic == ND - 1)):
                            ins = e.matmul(pvps[cc][:, 0:512], yqt[:, ic, :],
                                           wft[:, ic, cc * 512:(cc + 1) * 512],
                                           start=(ic == 0), stop=(ic == ND - 1))
                            if last:
                                ins.then_inc(sem["pe"], 1)
                        op("tensor", f)
                    pe_ffn_c[i][cc] = bump("pe")
                pe_ffn[i] = pe_ffn_c[i][1]

            def emit_ffnstt(i):
                par = i % 2
                op("vector", lambda e, par=par: e.tensor_add(sq[par][:, :], y1[par][:, :], biasb[:, :]))
                op("vector", lambda e: e.drain())
                if i >= 2:
                    wait("vector", f"do{i % 2}", do_store[i - 2])
                for cc in range(2):
                    wait("vector", "pe", pe_ffn_c[i][cc])
                    def f(e, cc=cc, par=par, last=(cc == 1)):
                        ins = e.scalar_tensor_tensor(t2[par][:, cc * 512:(cc + 1) * 512],
                                                     pvps[cc][:, 0:512], ssb[:, 0:1],
                                                     sq[par][:, cc * 512:(cc + 1) * 512],
                                                     ALU.mult, ALU.add,
                                                     accum_out=stat(ST2 if cc == 0 else ST2B, par))
                        if last:
                            ins.then_inc(sem["vf"], 1)
                    op("vector", f)
                ve_ffnevac[i] = bump("vf")
                pv_free_ve[0] = ("vf", ve_ffnevac[i])
                op("vector", lambda e: e.drain())
                op("vector", lambda e, par=par: e.tensor_scalar(stat(NM2, par), stat(ST2, par),
                                                                stat(ST2B, par), -1.0 / 1024.0,
                                                                ALU.add, ALU.mult).then_inc(sem["vf"], 1))
                ve_nm2[i] = bump("vf")

            def emit_ln2y2(i):
                par = i % 2
                wait("scalar", "vf", ve_nm2[i])
                op("scalar", lambda e, par=par: e.activation(sq[par][:, :], t2[par][:, :], AF.Square,
                                                             bias=stat(NM2, par), accum_out=stat(VS2, par)))
                op("scalar", lambda e: e.drain())
                op("scalar", lambda e, par=par: e.activation(stat(U1, par), stat(VS2, par), AF.Ln,
                                                             scale=cst[:, 1:2], bias=cst[:, 0:1]))
                op("scalar", lambda e: e.drain())
                op("scalar", lambda e, par=par: e.activation(stat(RS2, par), stat(U1, par), AF.Exp,
                                                             scale=cst[:, 2:3]).then_inc(sem["ac"], 1))
                ac_rstd2[i] = bump("ac")
                wait("vector", "ac", ac_rstd2[i])
                op("vector", lambda e, par=par: e.tensor_scalar(y2[par][:, :], t2[par][:, :],
                                                                stat(NM2, par), stat(RS2, par),
                                                                ALU.add, ALU.mult).then_inc(sem["vf"], 1))
                ve_y2[i] = bump("vf")
                wait("sync", "vf", ve_y2[i])
                op("sync", lambda e, i=i, par=par: e.dma_start(out=y_d[i * 128:(i + 1) * 128, :], in_=y2[par][:, :])
                   .then_inc(sem[f"do{par}"], 16))
                do_store[i] = bump(f"do{par}", 16)

            if do_phase2 and n_win > 0:
                wait("gpsimd", "pe", pe_phase1_done)
                wait("gpsimd", "ve", ve_phase1_done)
                emit_xr(0)
                if n_win > 1:
                    emit_xr(1)
                emit_S(0)
                emit_scopy(0)
                emit_diag(0)
                emit_max(0)
                emit_exp(0)
                emit_recip(0)
                emit_T(0)
                for i in range(n_win):
                    if i + 1 < n_win:
                        emit_S(i + 1)
                        emit_scopy(i + 1)
                        emit_diag(i + 1)
                        emit_max(i + 1)
                        emit_exp(i + 1)
                    emit_PV(i)
                    emit_pvstt(i)
                    if i >= 1:
                        emit_yT(i - 1)
                        emit_ytevac(i - 1)
                    emit_ln1(i)
                    if i + 2 < n_win:
                        emit_xr(i + 2)
                    if i >= 1:
                        emit_FFN(i - 1)
                        emit_ffnstt(i - 1)
                    emit_y1q(i)
                    if i + 1 < n_win:
                        emit_recip(i + 1)
                        emit_T(i + 1)
                    if i >= 1:
                        emit_ln2y2(i - 1)
                emit_yT(n_win - 1)
                emit_ytevac(n_win - 1)
                emit_FFN(n_win - 1)
                emit_ffnstt(n_win - 1)
                emit_ln2y2(n_win - 1)
                wait("sync", "do0", cnt["do0"])
                wait("sync", "do1", cnt["do1"])
            else:
                wait("vector", "ve", ve_phase1_done)
                op("vector", lambda e: e.tensor_copy(t2[0][:, :], ct[:, 0, 0:1024]).then_inc(sem["ve"], 1))
                vend = bump("ve")
                wait("sync", "ve", vend)
                op("sync", lambda e: e.dma_start(out=y_d[0:128, :], in_=t2[0][:, :]).then_inc(sem["do0"], 16))
                bump("do0", 16)
                wait("sync", "do0", cnt["do0"])

        block = ctx.enter_context(nc.Block())

        def runner(name):
            def run(eng):
                for e, fn in sched:
                    if e == name:
                        fn(eng)
            return run
        block.sync(runner("sync"))
        block.gpsimd(runner("gpsimd"))
        block.tensor(runner("tensor"))
        block.vector(runner("vector"))
        block.scalar(runner("scalar"))

    return nc


_NC_CACHE = [None]


def kernel(L0_attn_x, L0_attn_W, L0_ffn_weight_fp8, L0_ffn_bias, L0_ffn_input_scale, L0_ffn_weight_scale):
    x = np.ascontiguousarray(np.asarray(L0_attn_x, dtype=np.float32))
    W = np.ascontiguousarray(np.asarray(L0_attn_W, dtype=np.float32))
    Wf = np.ascontiguousarray(np.asarray(L0_ffn_weight_fp8, dtype=np.float32))
    bias = np.ascontiguousarray(np.asarray(L0_ffn_bias, dtype=np.float32).reshape(1, D))
    scl = np.ascontiguousarray(np.stack([np.asarray(L0_ffn_input_scale, dtype=np.float32).reshape(()),
                                         np.asarray(L0_ffn_weight_scale, dtype=np.float32).reshape(())]).reshape(1, 2))
    r = np.arange(128)
    mask = np.where(r[None, :] <= r[:, None], np.float32(0.0), np.float32(MASK_NEG)).astype(np.float32)
    idn = np.eye(128, dtype=ml_dtypes.bfloat16)

    if _NC_CACHE[0] is None:
        _NC_CACHE[0] = build_nc()
    nc = _NC_CACHE[0]
    in_maps = [{"x": x[b], "w": W, "wf": Wf, "bias": bias, "scl": scl, "mask": mask, "idn": idn}
               for b in range(8)]
    res = run_bass_kernel_spmd(nc, in_maps, core_ids=list(range(8)))
    return np.stack([res.results[b]["y"] for b in range(8)], axis=0)




# version 29
# speedup vs baseline: 1.0732x; 1.0732x over previous
import math
import numpy as np
import ml_dtypes
from contextlib import ExitStack
import concourse.bass as bass
import concourse.mybir as mybir
from concourse.bass_utils import run_bass_kernel_spmd

F32 = mybir.dt.float32
BF16 = mybir.dt.bfloat16
FP8 = mybir.dt.float8e4
AF = mybir.ActivationFunctionType
ALU = mybir.AluOpType
AXX = mybir.AxisListType.X

S_LEN = 2048
D = 1024
NT = 16
ND = 8
MASK_NEG = -1.0e4


def build_nc(n_win=NT, do_phase2=True, last_parts=3, skip=()):
    nc = bass.Bass("TRN2", target_bir_lowering=False)

    x_d = nc.dram_tensor("x", [S_LEN, D], F32, kind="ExternalInput").ap()
    w_d = nc.dram_tensor("w", [2 * D, D], F32, kind="ExternalInput").ap()
    wf_d = nc.dram_tensor("wf", [D, D], F32, kind="ExternalInput").ap()
    bias_d = nc.dram_tensor("bias", [1, D], F32, kind="ExternalInput").ap()
    scl_d = nc.dram_tensor("scl", [1, 2], F32, kind="ExternalInput").ap()
    mask_d = nc.dram_tensor("mask", [128, 128], F32, kind="ExternalInput").ap()
    idn_d = nc.dram_tensor("idn", [128, 128], BF16, kind="ExternalInput").ap()
    y_d = nc.dram_tensor("y", [S_LEN, D], F32, kind="ExternalOutput").ap()
    wvbf_d = nc.dram_tensor("wvbf_scratch", [D, D], BF16).ap()
    wfbf_d = nc.dram_tensor("wfbf_scratch", [D, D], BF16).ap()

    sched = []

    def op(eng, fn):
        sched.append((eng, fn))

    cnt = {"pe": 0, "ve": 0, "ac": 0, "ds": 0, "vs": 0, "vt": 0, "vp": 0, "vf": 0,
           "xs0": 0, "xs1": 0, "xs2": 0, "xs3": 0, "wks": 0, "wvs": 0, "wfs": 0,
           "xr0": 0, "xr1": 0, "xr2": 0, "xr3": 0, "do0": 0, "do1": 0, "po": 0,
           "xfs": 0, "wts": 0, "fts": 0, "qs0": 0}

    ctx = ExitStack()
    with ctx:
        sem = {k: ctx.enter_context(nc.semaphore(k)) for k in cnt}

        def wait(eng, key, val):
            if val <= 0:
                return
            op(eng, lambda e, s=sem[key], v=val: e.wait_ge(s, v))

        def bump(key, k=1):
            cnt[key] += k
            return cnt[key]

        xfm = ctx.enter_context(nc.sbuf_tensor("xfm", [128, ND, S_LEN], BF16))
        ct = ctx.enter_context(nc.sbuf_tensor("ct", [128, ND, S_LEN], BF16))
        vtm = ctx.enter_context(nc.sbuf_tensor("vtm", [128, NT, D], BF16))
        wft = ctx.enter_context(nc.sbuf_tensor("wft", [128, ND, D], BF16))
        biasb = ctx.enter_context(nc.sbuf_tensor("biasb", [128, D], F32))
        maskd = ctx.enter_context(nc.sbuf_tensor("maskd", [128, 128], F32))
        idn = ctx.enter_context(nc.sbuf_tensor("idn_sb", [128, 128], BF16))
        ones = ctx.enter_context(nc.sbuf_tensor("ones", [1, 128], F32))
        bias_sb = ctx.enter_context(nc.sbuf_tensor("bias_sb", [1, D], F32))
        scl_sb = ctx.enter_context(nc.sbuf_tensor("scl_sb", [1, 2], F32))
        sclb = ctx.enter_context(nc.sbuf_tensor("sclb", [128, 2], F32))
        invh = ctx.enter_context(nc.sbuf_tensor("invh", [128, 1], F32))
        ssb = ctx.enter_context(nc.sbuf_tensor("ssb", [128, 1], F32))
        cst = ctx.enter_context(nc.sbuf_tensor("cst", [128, 3], F32))
        stats = ctx.enter_context(nc.sbuf_tensor("stats", [128, 30], F32))
        pmax = ctx.enter_context(nc.sbuf_tensor("pmax", [128, 8], F32))

        def stat(k, par):
            return stats[:, 2 * k + par: 2 * k + par + 1]
        NMAX, LSUM, RECI, ST1, NM1, VS1, U1, RS1, ST2, NM2, VS2, RS2, ST1B, ST2B = range(14)
        LSUMB = 13
        LSUMB = 14

        op("sync", lambda e: e.dma_start(out=bias_sb[:], in_=bias_d).then_inc(sem["ds"], 16))
        op("sync", lambda e: e.dma_start(out=scl_sb[:], in_=scl_d).then_inc(sem["ds"], 16))
        op("sync", lambda e: e.dma_start(out=maskd[:], in_=mask_d).then_inc(sem["ds"], 16))
        op("sync", lambda e: e.dma_start(out=idn[:], in_=idn_d).then_inc(sem["ds"], 16))
        ds_consts = 64

        op("vector", lambda e: e.memset(cst[:, 0:1], 1e-5))
        op("vector", lambda e: e.memset(cst[:, 1:2], 1.0 / 1024.0))
        op("vector", lambda e: e.memset(cst[:, 2:3], -0.5))
        op("vector", lambda e: e.memset(ones[:, :], 1.0).then_inc(sem["ve"], 1))
        ve_init = bump("ve")

        def _preload(e):
            try:
                from concourse.hw_specs import get_activation_tables
                need = {AF.Exp, AF.Ln, AF.Copy, AF.Square}
                for idx, funcs in enumerate(get_activation_tables(e.bass.m.arch).values()):
                    if need.issubset(funcs):
                        e.add_instruction(mybir.InstLoadActFuncSet(
                            name=e.bass.get_next_instruction_name(),
                            act_func_set_id=idx, ins=[], outs=[]))
                        return
            except Exception:
                pass
        op("scalar", _preload)

        ctx1 = ExitStack()
        with ctx1:
            wbufA = ctx1.enter_context(nc.sbuf_tensor("wbufA", [128, ND, D], BF16))
            wvt = ctx1.enter_context(nc.sbuf_tensor("wvt", [128, ND, D], BF16))
            xtm = ctx1.enter_context(nc.sbuf_tensor("xtm", [128, NT, D], BF16))
            trp = [ctx1.enter_context(nc.psum_tensor(f"trp{p}", [128, ND, 128], BF16)) for p in range(2)]
            cps = [ctx1.enter_context(nc.psum_tensor(f"cps{c}", [128, 512], F32)) for c in range(4)]

            xchunks = [(0, 1), (1, 4), (4, 10), (10, 16)]
            for c, (t0, t1) in enumerate(xchunks):
                def f(e, c=c, t0=t0, t1=t1):
                    e.dma_start(out=xtm[:, t0:t1, :],
                                in_=x_d[t0 * 128:t1 * 128, :].rearrange("(t p) d -> p t d", p=128)
                                ).then_inc(sem[f"xs{c}"], 16)
                op("gpsimd", f)
                bump(f"xs{c}", 16)
            op("gpsimd", lambda e: e.dma_start(out=wbufA[:, :, :],
                                               in_=w_d[0:D, :].rearrange("(t p) d -> p t d", p=128))
               .then_inc(sem["wks"], 16))
            bump("wks", 16)
            op("gpsimd", lambda e: e.dma_start(out=wvbf_d[:, :], in_=w_d[D:2 * D, :])
               .then_inc(sem["wvs"], 16))
            bump("wvs", 16)
            op("gpsimd", lambda e: e.dma_start(out=wfbf_d[:, :], in_=wf_d)
               .then_inc(sem["wfs"], 16))
            bump("wfs", 16)

            ve_xt = [0] * NT
            pe_xt = [0] * NT
            xwait = {t0: c for c, (t0, t1) in enumerate(xchunks)}
            for t in range(NT):
                if t in xwait:
                    wait("tensor", f"xs{xwait[t]}", 16)
                if t == 0:
                    wait("tensor", "ds", ds_consts)
                if t >= 2:
                    wait("tensor", "ve", ve_xt[t - 2])
                for d in range(ND):
                    def f(e, t=t, d=d, last=(d == ND - 1)):
                        ins = e.transpose(trp[t % 2][:, d, :], xtm[:, t, d * 128:(d + 1) * 128], idn[:])
                        if last:
                            ins.then_inc(sem["pe"], 1)
                    op("tensor", f)
                pe_xt[t] = bump("pe")
                wait("vector", "pe", pe_xt[t])
                op("vector", lambda e, t=t: e.tensor_copy(xfm[:, :, t * 128:(t + 1) * 128], trp[t % 2][:, :, :])
                   .then_inc(sem["ve"], 1))
                ve_xt[t] = bump("ve")

            wait("sync", "wvs", 16)
            for dt in range(ND):
                op("sync", lambda e, dt=dt: e.dma_start_transpose(
                    wvt[:, dt, :], wvbf_d[:, dt * 128:(dt + 1) * 128]).then_inc(sem["wts"], 16))
                bump("wts", 16)
            wait("sync", "wfs", 16)
            for it in range(ND):
                op("sync", lambda e, it=it: e.dma_start_transpose(
                    wft[:, it, :], wfbf_d[:, it * 128:(it + 1) * 128]).then_inc(sem["fts"], 16))
                bump("fts", 16)

            wait("tensor", "ds", ds_consts)
            wait("tensor", "ve", ve_init)
            op("tensor", lambda e: e.matmul(cps[0][:, 0:2], ones[:, :], scl_sb[:, :], start=True, stop=True))
            op("tensor", lambda e: e.matmul(cps[1][:, 0:512], ones[:, :], bias_sb[:, 0:512], start=True, stop=True))
            op("tensor", lambda e: e.matmul(cps[2][:, 0:512], ones[:, :], bias_sb[:, 512:1024], start=True, stop=True)
               .then_inc(sem["pe"], 1))
            pe_bcast = bump("pe")
            wait("vector", "pe", pe_bcast)
            op("vector", lambda e: e.tensor_copy(sclb[:, :], cps[0][:, 0:2]))
            op("vector", lambda e: e.drain())
            op("vector", lambda e: e.reciprocal(invh[:, :], sclb[:, 0:1]))
            op("vector", lambda e: e.drain())
            op("vector", lambda e: e.tensor_scalar_mul(invh[:, :], invh[:, :], 0.5))
            op("vector", lambda e: e.tensor_mul(ssb[:, :], sclb[:, 0:1], sclb[:, 1:2]))
            op("vector", lambda e: e.tensor_copy(biasb[:, 0:512], cps[1][:, 0:512]))
            op("vector", lambda e: e.tensor_copy(biasb[:, 512:1024], cps[2][:, 0:512])
               .then_inc(sem["ve"], 1))
            ve_bcast = bump("ve")

            wait("tensor", "ve", ve_xt[NT - 1])
            wait("tensor", "wks", 16)
            wait("tensor", "ve", ve_bcast)
            ve_ct_half = {}
            kglob = 0
            for dt in range(ND):
                for h in range(2):
                    pair = kglob % 2
                    if kglob >= 2:
                        wait("tensor", "ve", ve_ct_half[kglob - 2])
                    for j in range(ND):
                        for cc in range(2):
                            c = h * 2 + cc
                            bank = cps[2 * pair + cc]
                            def f(e, dt=dt, j=j, c=c, bank=bank, last=(j == ND - 1 and cc == 1)):
                                ins = e.matmul(bank[:, 0:512],
                                               wbufA[:, j, dt * 128:(dt + 1) * 128],
                                               xfm[:, j, c * 512:(c + 1) * 512],
                                               start=(j == 0), stop=(j == ND - 1))
                                if last:
                                    ins.then_inc(sem["pe"], 1)
                            op("tensor", f)
                    pe_h = bump("pe")
                    wait("vector", "pe", pe_h)
                    for cc in range(2):
                        c = h * 2 + cc
                        bank = cps[2 * pair + cc]
                        def f(e, dt=dt, c=c, bank=bank, last=(cc == 1)):
                            ins = e.tensor_copy(ct[:, dt, c * 512:(c + 1) * 512], bank[:, 0:512])
                            if last:
                                ins.then_inc(sem["ve"], 1)
                        op("vector", f)
                    ve_ct_half[kglob] = bump("ve")
                    kglob += 1
            pe_ct_done = cnt["pe"]
            ve_ct_done = cnt["ve"]

            wait("tensor", "wts", 128)
            ve_v = [0] * NT
            for st in range(NT):
                pair = st % 2
                if st >= 2:
                    wait("tensor", "ve", ve_v[st - 2])
                elif st == 0:
                    wait("tensor", "ve", ve_ct_done)
                for d in range(ND):
                    for cc in range(2):
                        def f(e, st=st, d=d, cc=cc, pair=pair, last=(d == ND - 1 and cc == 1)):
                            ins = e.matmul(cps[2 * pair + cc][:, 0:512],
                                           xfm[:, d, st * 128:(st + 1) * 128],
                                           wvt[:, d, cc * 512:(cc + 1) * 512],
                                           start=(d == 0), stop=(d == ND - 1))
                            if last:
                                ins.then_inc(sem["pe"], 1)
                        op("tensor", f)
                pe_v = bump("pe")
                wait("vector", "pe", pe_v)
                for cc in range(2):
                    def f(e, st=st, cc=cc, pair=pair, last=(cc == 1)):
                        ins = e.tensor_copy(vtm[:, st, cc * 512:(cc + 1) * 512], cps[2 * pair + cc][:, 0:512])
                        if last:
                            ins.then_inc(sem["ve"], 1)
                    op("vector", f)
                ve_v[st] = bump("ve")

            wait("tensor", "fts", 128)
            ve_phase1_done = cnt["ve"]
            pe_phase1_done = cnt["pe"]

        ctx2 = ExitStack()
        with ctx2:
            sps = [ctx2.enter_context(nc.psum_tensor(f"sps{c}", [128, 512], F32)) for c in range(4)]
            pvps = [ctx2.enter_context(nc.psum_tensor(f"pvps{c}", [128, 512], F32)) for c in range(2)]
            trp2 = [ctx2.enter_context(nc.psum_tensor(f"trp2_{p}", [128, 8, 128], BF16)) for p in range(2)]

            s_sb = [ctx2.enter_context(nc.sbuf_tensor(f"s_sb{p}", [128, S_LEN], F32)) for p in range(2)]
            p_bf = [ctx2.enter_context(nc.sbuf_tensor(f"p_bf{p}", [128, S_LEN], BF16)) for p in range(2)]
            pt = ctx2.enter_context(nc.sbuf_tensor("pt", [128, NT, 128], BF16))
            xr = [ctx2.enter_context(nc.sbuf_tensor(f"xr{p}", [128, D], F32)) for p in range(2)]
            t_sb = [ctx2.enter_context(nc.sbuf_tensor(f"t_sb{p}", [128, D], F32)) for p in range(2)]
            y1 = [ctx2.enter_context(nc.sbuf_tensor(f"y1_{p}", [128, D], F32)) for p in range(2)]
            sq = [ctx2.enter_context(nc.sbuf_tensor(f"sq{p}", [128, D], F32)) for p in range(2)]
            q8 = [ctx2.enter_context(nc.sbuf_tensor("q8_0", [128, D], FP8))] * 2
            yqb2 = [ctx2.enter_context(nc.sbuf_tensor("yqb2_0", [128, D], BF16))] * 2
            yqb = [ctx2.enter_context(nc.sbuf_tensor(f"yqb{p}", [128, D], BF16)) for p in range(2)]
            yqt = ctx2.enter_context(nc.sbuf_tensor("yqt", [128, ND, 128], BF16))
            t2 = [ctx2.enter_context(nc.sbuf_tensor(f"t2_{p}", [128, D], F32)) for p in range(2)]
            y2 = [ctx2.enter_context(nc.sbuf_tensor(f"y2_{p}", [128, D], F32)) for p in range(2)]



            dg_xr = [0] * NT
            ve_pvevac = [0] * NT
            ac_scopy = [0] * NT
            ve_diag = [0] * NT
            ve_max = [0] * NT
            ac_exp = [0] * NT
            ve_tevac = [0] * NT
            pe_s = [0] * NT
            pe_pv = [0] * NT
            pe_yt = [0] * NT
            pe_ffn = [0] * NT
            ve_sevac_done = [0] * NT
            ve_ytevac = [0] * NT
            ve_ffnevac = [0] * NT
            ve_y2 = [0] * NT
            ve_q = [0] * NT
            ac_rstd1 = [0] * NT
            ac_rstd2 = [0] * NT
            do_store = [0] * NT
            ve_nm1 = [0] * NT
            ve_pvstt_c = [[0, 0] for _ in range(NT)]
            pe_ffn_c = [[0, 0] for _ in range(NT)]
            ve_scopyd = [0] * NT
            tev_groups = [None] * NT
            pe_s_chunk = [None] * NT
            ac_scopy_c = [[0] * 4 for _ in range(NT)]
            pe_yt_a = [0] * NT
            ve_nm2 = [0] * NT
            ac_exp_a = [0] * NT
            exp_split = [False] * NT
            ve_ytev_a = [0] * NT
            ve_y1 = [0] * NT
            yt_tb = [0] * NT
            trp2_free2 = [("ac", 0), ("ac", 0)]
            tb_ctr = 0

            def emit_xr(i):
                nonlocal dg_xr
                k = f"xr{i % 4}"
                if i >= 2:
                    wait("gpsimd", "vp", ve_pvevac[i - 2])
                op("gpsimd", lambda e, i=i, k=k: e.dma_start(out=xr[i % 2][:, :], in_=x_d[i * 128:(i + 1) * 128, :])
                   .then_inc(sem[k], 16))
                v1 = bump(k, 16)
                wait("gpsimd", k, v1)
                op("gpsimd", lambda e, i=i, k=k: e.dma_start(out=xr[i % 2][:, :], in_=x_d[i * 128:(i + 1) * 128, :],
                                                             accum_op=ALU.add).then_inc(sem[k], 16))
                dg_xr[i] = bump(k, 16)

            def emit_S(i):
                par = i % 2
                nch = (i + 1) * 128 // 512 + (1 if ((i + 1) * 128) % 512 else 0)
                if i == 0:
                    wait("tensor", "ve", ve_phase1_done)
                else:
                    wait("tensor", "ac", ac_scopy[i - 1])
                pe_s_chunk[i] = [0] * nch
                for c in range(nch):
                    w = min(512, (i + 1) * 128 - c * 512)
                    for j in range(ND):
                        def f(e, i=i, j=j, c=c, w=w, last=(j == ND - 1)):
                            ins = e.matmul(sps[c][:, 0:w],
                                           ct[:, j, i * 128:(i + 1) * 128],
                                           xfm[:, j, c * 512:c * 512 + w],
                                           start=(j == 0), stop=(j == ND - 1))
                            if last:
                                ins.then_inc(sem["pe"], 1)
                        op("tensor", f)
                    pe_s_chunk[i][c] = bump("pe")
                pe_s[i] = pe_s_chunk[i][-1]

            def emit_scopy(i):
                par = i % 2
                W = (i + 1) * 128
                nch = W // 512 + (1 if W % 512 else 0)
                if i >= 2:
                    wait("scalar", "vs", ve_max[i - 2])
                for c in range(nch):
                    w = min(512, W - c * 512)
                    wait("scalar", "pe", pe_s_chunk[i][c])
                    def f(e, c=c, w=w, par=par):
                        e.copy(s_sb[par][:, c * 512:c * 512 + w], sps[c][:, 0:w]).then_inc(sem["ac"], 1)
                    op("scalar", f)
                    ac_scopy_c[i][c] = bump("ac")
                ac_scopy[i] = ac_scopy_c[i][nch - 1]
                ve_scopyd[i] = 0

            def emit_diag(i):
                par = i % 2
                W = (i + 1) * 128
                nch = W // 512 + (1 if W % 512 else 0)
                dcol = i * 128
                if i >= 2:
                    wait("vector", "ac", ac_exp[i - 2])
                for c in range(nch - 1):
                    wait("vector", "ac", ac_scopy_c[i][c])
                    def f(e, c=c, par=par):
                        e.reduce_max(pmax[:, c:c + 1], s_sb[par][:, c * 512:(c + 1) * 512],
                                     axis=AXX, negate=True)
                    op("vector", f)
                wait("vector", "ac", ac_scopy_c[i][nch - 1])
                op("vector", lambda e: e.drain())
                op("vector", lambda e, par=par, dcol=dcol: e.tensor_add(
                    s_sb[par][:, dcol:dcol + 128], s_sb[par][:, dcol:dcol + 128], maskd[:])
                    .then_inc(sem["vs"], 1))
                ve_diag[i] = bump("vs")

            def emit_max(i):
                par = i % 2
                W = (i + 1) * 128
                nch = W // 512 + (1 if W % 512 else 0)
                w = W - (nch - 1) * 512
                op("vector", lambda e: e.drain())
                def f(e, c=nch - 1, w=w, par=par):
                    e.reduce_max(pmax[:, c:c + 1], s_sb[par][:, c * 512:c * 512 + w],
                                 axis=AXX, negate=True)
                op("vector", f)
                op("vector", lambda e: e.drain())
                op("vector", lambda e, par=par, nch=nch: e.tensor_reduce(
                    stat(NMAX, par), pmax[:, 0:nch], AXX, ALU.min).then_inc(sem["vs"], 1))
                ve_max[i] = bump("vs")
            def emit_exp(i):
                par = i % 2
                W = (i + 1) * 128
                Wa = min(W, 1024)
                wait("scalar", "vs", ve_max[i])
                if i >= 2:
                    wait("scalar", "pe", pe_t_last[i - 2])
                op("scalar", lambda e, par=par, Wa=Wa: e.activation(
                    p_bf[par][:, 0:Wa], s_sb[par][:, 0:Wa], AF.Exp,
                    bias=stat(NMAX, par), accum_out=stat(LSUM, par)).then_inc(sem["ac"], 1))
                ac_exp_a[i] = bump("ac")
                if W > 1024:
                    op("scalar", lambda e: e.drain())
                    op("scalar", lambda e, par=par, W=W: e.activation(
                        p_bf[par][:, 1024:W], s_sb[par][:, 1024:W], AF.Exp,
                        bias=stat(NMAX, par), accum_out=stat(LSUMB, par)).then_inc(sem["ac"], 1))
                    ac_exp[i] = bump("ac")
                else:
                    ac_exp[i] = ac_exp_a[i]
                exp_split[i] = (W > 1024)

            def emit_recip(i):
                par = i % 2
                wait("vector", "ac", ac_exp[i])
                if exp_split[i]:
                    op("vector", lambda e, par=par: e.tensor_add(stat(LSUM, par), stat(LSUM, par),
                                                                 stat(LSUMB, par)))
                    op("vector", lambda e: e.drain())
                op("vector", lambda e, par=par: e.reciprocal(stat(RECI, par), stat(LSUM, par))
                   .then_inc(sem["vs"], 1))
                op("vector", lambda e: e.drain())
                bump("vs")

            pe_t_last = [0] * NT

            def emit_T(i):
                nonlocal tb_ctr
                ntile = i + 1
                par = i % 2
                col = 0
                evs = []
                while col < ntile:
                    ng = min(4, ntile - col)
                    tb = tb_ctr % 2
                    tb_ctr += 1
                    wait("tensor", trp2_free2[tb][0], trp2_free2[tb][1])
                    if col < 8:
                        wait("tensor", "ac", ac_exp_a[i])
                    else:
                        wait("tensor", "ac", ac_exp[i])
                    for u in range(ng):
                        def f(e, par=par, col=col, u=u, tb=tb, last=(u == ng - 1)):
                            ins = e.transpose(trp2[tb][:, u, :],
                                              p_bf[par][:, (col + u) * 128:(col + u + 1) * 128], idn[:])
                            if last:
                                ins.then_inc(sem["pe"], 1)
                        op("tensor", f)
                    pe_g = bump("pe")
                    wait("scalar", "pe", pe_g)
                    op("scalar", lambda e, col=col, ng=ng, tb=tb: e.copy(
                        pt[:, col:col + ng, :], trp2[tb][:, 0:ng, :]).then_inc(sem["ac"], 1))
                    trp2_free2[tb] = ("ac", bump("ac"))
                    evs.append(trp2_free2[tb][1])
                    col += ng
                pe_t_last[i] = pe_g
                tev_groups[i] = list(evs)
                ve_tevac[i] = evs[-1]

            pv_free_ve = [("vp", 0)]

            def emit_PV(i):
                par = i % 2
                wait("tensor", pv_free_ve[0][0], pv_free_ve[0][1])
                for t in range(i + 1):
                    if t % 4 == 0:
                        wait("tensor", "ac", tev_groups[i][t // 4])
                    for cc in range(2):
                        def f(e, t=t, cc=cc, i=i, last=(t == i and cc == 1)):
                            ins = e.matmul(pvps[cc][:, 0:512], pt[:, t, :],
                                           vtm[:, t, cc * 512:(cc + 1) * 512],
                                           start=(t == 0), stop=(t == i))
                            if last:
                                ins.then_inc(sem["pe"], 1)
                        op("tensor", f)
                pe_pv[i] = bump("pe")

            def emit_pvstt(i):
                par = i % 2
                wait("vector", "pe", pe_pv[i])
                wait("vector", f"xr{i % 4}", dg_xr[i])
                for cc in range(2):
                    def f(e, cc=cc, par=par):
                        e.scalar_tensor_tensor(t_sb[par][:, cc * 512:(cc + 1) * 512],
                                               pvps[cc][:, 0:512], stat(RECI, par),
                                               xr[par][:, cc * 512:(cc + 1) * 512],
                                               ALU.mult, ALU.add,
                                               accum_out=stat(ST1 if cc == 0 else ST1B, par)
                                               ).then_inc(sem["vp"], 1)
                    op("vector", f)
                    ve_pvstt_c[i][cc] = bump("vp")
                ve_pvevac[i] = ve_pvstt_c[i][1]
                pv_free_ve[0] = ("vp", ve_pvevac[i])
                op("vector", lambda e: e.drain())
                op("vector", lambda e, par=par: e.tensor_scalar(stat(NM1, par), stat(ST1, par),
                                                                stat(ST1B, par), -1.0 / 1024.0,
                                                                ALU.add, ALU.mult).then_inc(sem["vp"], 1))
                ve_nm1[i] = bump("vp")

            def emit_ln1(i):
                par = i % 2
                wait("scalar", "vp", ve_nm1[i])
                op("scalar", lambda e, par=par: e.activation(sq[par][:, :], t_sb[par][:, :], AF.Square,
                                                             bias=stat(NM1, par), accum_out=stat(VS1, par)))
                op("scalar", lambda e: e.drain())
                op("scalar", lambda e, par=par: e.activation(stat(U1, par), stat(VS1, par), AF.Ln,
                                                             scale=cst[:, 1:2], bias=cst[:, 0:1]))
                op("scalar", lambda e: e.drain())
                op("scalar", lambda e, par=par: e.activation(stat(RS1, par), stat(U1, par), AF.Exp,
                                                             scale=cst[:, 2:3]).then_inc(sem["ac"], 1))
                ac_rstd1[i] = bump("ac")

            def emit_y1q(i):
                par = i % 2
                wait("vector", "ac", ac_rstd1[i])
                op("vector", lambda e, par=par: e.tensor_scalar(y1[par][:, :], t_sb[par][:, :],
                                                                stat(NM1, par), stat(RS1, par),
                                                                ALU.add, ALU.mult).then_inc(sem["vp"], 1))
                ve_y1[i] = bump("vp")
                op("vector", lambda e: e.drain())
                op("vector", lambda e, par=par: e.tensor_scalar(sq[par][:, :], y1[par][:, :],
                                                                invh[:, 0:1], 224.0, ALU.mult, ALU.min))
                op("vector", lambda e: e.drain())
                op("vector", lambda e, par=par: e.tensor_scalar(q8[par][:, :], sq[par][:, :],
                                                                -224.0, None, ALU.max))
                op("vector", lambda e: e.drain())
                if i >= 2:
                    wait("vector", "qs0", 16 * (i - 1))
                op("vector", lambda e, par=par: e.tensor_scalar(yqb[par][:, :], q8[par][:, :],
                                                                2.0, None, ALU.mult).then_inc(sem["vp"], 1))
                ve_q[i] = bump("vp")
                wait("sync", "vp", ve_q[i])
                if i >= 1:
                    wait("sync", "pe", pe_yt[i - 1])
                op("sync", lambda e, par=par: e.dma_start(
                    out=yqb2[par][:, :].bitcast(mybir.dt.int16),
                    in_=yqb[par][:, :].bitcast(mybir.dt.int16)).then_inc(sem["qs0"], 16))
                bump("qs0", 16)

            def emit_yT(i):
                nonlocal tb_ctr
                par = i % 2
                tb1 = tb_ctr % 2
                tb_ctr += 1
                tb2 = tb_ctr % 2
                tb_ctr += 1
                yt_tb[i] = (tb1, tb2)
                wait("tensor", "qs0", 16 * (i + 1))
                wait("tensor", trp2_free2[tb1][0], trp2_free2[tb1][1])
                for u in range(4):
                    def f(e, par=par, u=u, tb=tb1, last=(u == 3)):
                        ins = e.transpose(trp2[tb][:, u, :], yqb2[par][:, u * 128:(u + 1) * 128], idn[:])
                        if last:
                            ins.then_inc(sem["pe"], 1)
                    op("tensor", f)
                pe_yt_a[i] = bump("pe")
                wait("tensor", trp2_free2[tb2][0], trp2_free2[tb2][1])
                for u in range(4, 8):
                    def f(e, par=par, u=u, tb=tb2, last=(u == 7)):
                        ins = e.transpose(trp2[tb][:, u - 4, :], yqb2[par][:, u * 128:(u + 1) * 128], idn[:])
                        if last:
                            ins.then_inc(sem["pe"], 1)
                    op("tensor", f)
                pe_yt[i] = bump("pe")

            def emit_ytevac(i):
                tb1, tb2 = yt_tb[i]
                wait("scalar", "pe", pe_yt_a[i])
                op("scalar", lambda e, tb=tb1: e.copy(yqt[:, 0:4, :], trp2[tb][:, 0:4, :])
                   .then_inc(sem["ac"], 1))
                trp2_free2[tb1] = ("ac", bump("ac"))
                ve_ytev_a[i] = trp2_free2[tb1][1]
                wait("scalar", "pe", pe_yt[i])
                op("scalar", lambda e, tb=tb2: e.copy(yqt[:, 4:8, :], trp2[tb][:, 0:4, :])
                   .then_inc(sem["ac"], 1))
                trp2_free2[tb2] = ("ac", bump("ac"))
                ve_ytevac[i] = trp2_free2[tb2][1]

            def emit_FFN(i):
                wait("tensor", "ac", ve_ytev_a[i])
                pe_ffn_c[i] = [0, 0]
                for cc in range(2):
                    wait("tensor", "vp", ve_pvstt_c[min(i + 1, NT - 1)][cc])
                    if cc == 0:
                        pass
                    else:
                        wait("tensor", "ac", ve_ytevac[i])
                    for ic in range(ND):
                        if cc == 0 and ic == 4:
                            wait("tensor", "ac", ve_ytevac[i])
                        def f(e, ic=ic, cc=cc, last=(ic == ND - 1)):
                            ins = e.matmul(pvps[cc][:, 0:512], yqt[:, ic, :],
                                           wft[:, ic, cc * 512:(cc + 1) * 512],
                                           start=(ic == 0), stop=(ic == ND - 1))
                            if last:
                                ins.then_inc(sem["pe"], 1)
                        op("tensor", f)
                    pe_ffn_c[i][cc] = bump("pe")
                pe_ffn[i] = pe_ffn_c[i][1]

            def emit_ffnstt(i):
                par = i % 2
                op("vector", lambda e, par=par: e.tensor_add(sq[par][:, :], y1[par][:, :], biasb[:, :]))
                op("vector", lambda e: e.drain())
                if i >= 2:
                    wait("vector", f"do{i % 2}", do_store[i - 2])
                for cc in range(2):
                    wait("vector", "pe", pe_ffn_c[i][cc])
                    def f(e, cc=cc, par=par, last=(cc == 1)):
                        ins = e.scalar_tensor_tensor(t2[par][:, cc * 512:(cc + 1) * 512],
                                                     pvps[cc][:, 0:512], ssb[:, 0:1],
                                                     sq[par][:, cc * 512:(cc + 1) * 512],
                                                     ALU.mult, ALU.add,
                                                     accum_out=stat(ST2 if cc == 0 else ST2B, par))
                        if last:
                            ins.then_inc(sem["vf"], 1)
                    op("vector", f)
                ve_ffnevac[i] = bump("vf")
                pv_free_ve[0] = ("vf", ve_ffnevac[i])
                op("vector", lambda e: e.drain())
                op("vector", lambda e, par=par: e.tensor_scalar(stat(NM2, par), stat(ST2, par),
                                                                stat(ST2B, par), -1.0 / 1024.0,
                                                                ALU.add, ALU.mult).then_inc(sem["vf"], 1))
                ve_nm2[i] = bump("vf")

            def emit_ln2y2(i):
                par = i % 2
                wait("scalar", "vf", ve_nm2[i])
                op("scalar", lambda e, par=par: e.activation(sq[par][:, :], t2[par][:, :], AF.Square,
                                                             bias=stat(NM2, par), accum_out=stat(VS2, par)))
                op("scalar", lambda e: e.drain())
                op("scalar", lambda e, par=par: e.activation(stat(U1, par), stat(VS2, par), AF.Ln,
                                                             scale=cst[:, 1:2], bias=cst[:, 0:1]))
                op("scalar", lambda e: e.drain())
                op("scalar", lambda e, par=par: e.activation(stat(RS2, par), stat(U1, par), AF.Exp,
                                                             scale=cst[:, 2:3]).then_inc(sem["ac"], 1))
                ac_rstd2[i] = bump("ac")
                wait("vector", "ac", ac_rstd2[i])
                op("vector", lambda e, par=par: e.tensor_scalar(y2[par][:, :], t2[par][:, :],
                                                                stat(NM2, par), stat(RS2, par),
                                                                ALU.add, ALU.mult).then_inc(sem["vf"], 1))
                ve_y2[i] = bump("vf")
                wait("sync", "vf", ve_y2[i])
                op("sync", lambda e, i=i, par=par: e.dma_start(out=y_d[i * 128:(i + 1) * 128, :], in_=y2[par][:, :])
                   .then_inc(sem[f"do{par}"], 16))
                do_store[i] = bump(f"do{par}", 16)

            if do_phase2 and n_win > 0:
                wait("gpsimd", "pe", pe_phase1_done)
                wait("gpsimd", "ve", ve_phase1_done)
                emit_xr(0)
                if n_win > 1:
                    emit_xr(1)
                emit_S(0)
                emit_scopy(0)
                emit_diag(0)
                emit_max(0)
                emit_exp(0)
                emit_recip(0)
                emit_T(0)
                for i in range(n_win):
                    if i + 1 < n_win:
                        emit_S(i + 1)
                        emit_scopy(i + 1)
                        emit_diag(i + 1)
                        emit_max(i + 1)
                        emit_exp(i + 1)
                    emit_PV(i)
                    emit_pvstt(i)
                    if i >= 1:
                        emit_yT(i - 1)
                        emit_ytevac(i - 1)
                    emit_ln1(i)
                    if i + 2 < n_win:
                        emit_xr(i + 2)
                    if i >= 1:
                        emit_FFN(i - 1)
                        emit_ffnstt(i - 1)
                    emit_y1q(i)
                    if i + 1 < n_win:
                        emit_recip(i + 1)
                        emit_T(i + 1)
                    if i >= 1:
                        emit_ln2y2(i - 1)
                emit_yT(n_win - 1)
                emit_ytevac(n_win - 1)
                emit_FFN(n_win - 1)
                emit_ffnstt(n_win - 1)
                emit_ln2y2(n_win - 1)
                wait("sync", "do0", cnt["do0"])
                wait("sync", "do1", cnt["do1"])
            else:
                wait("vector", "ve", ve_phase1_done)
                op("vector", lambda e: e.tensor_copy(t2[0][:, :], ct[:, 0, 0:1024]).then_inc(sem["ve"], 1))
                vend = bump("ve")
                wait("sync", "ve", vend)
                op("sync", lambda e: e.dma_start(out=y_d[0:128, :], in_=t2[0][:, :]).then_inc(sem["do0"], 16))
                bump("do0", 16)
                wait("sync", "do0", cnt["do0"])

        block = ctx.enter_context(nc.Block())

        def runner(name):
            def run(eng):
                for e, fn in sched:
                    if e == name:
                        fn(eng)
            return run
        block.sync(runner("sync"))
        block.gpsimd(runner("gpsimd"))
        block.tensor(runner("tensor"))
        block.vector(runner("vector"))
        block.scalar(runner("scalar"))

    return nc


_NC_CACHE = [None]


def kernel(L0_attn_x, L0_attn_W, L0_ffn_weight_fp8, L0_ffn_bias, L0_ffn_input_scale, L0_ffn_weight_scale):
    x = np.ascontiguousarray(np.asarray(L0_attn_x, dtype=np.float32))
    W = np.ascontiguousarray(np.asarray(L0_attn_W, dtype=np.float32))
    Wf = np.ascontiguousarray(np.asarray(L0_ffn_weight_fp8, dtype=np.float32))
    bias = np.ascontiguousarray(np.asarray(L0_ffn_bias, dtype=np.float32).reshape(1, D))
    scl = np.ascontiguousarray(np.stack([np.asarray(L0_ffn_input_scale, dtype=np.float32).reshape(()),
                                         np.asarray(L0_ffn_weight_scale, dtype=np.float32).reshape(())]).reshape(1, 2))
    r = np.arange(128)
    mask = np.where(r[None, :] <= r[:, None], np.float32(0.0), np.float32(MASK_NEG)).astype(np.float32)
    idn = np.eye(128, dtype=ml_dtypes.bfloat16)

    if _NC_CACHE[0] is None:
        _NC_CACHE[0] = build_nc()
    nc = _NC_CACHE[0]
    in_maps = [{"x": x[b], "w": W, "wf": Wf, "bias": bias, "scl": scl, "mask": mask, "idn": idn}
               for b in range(8)]
    res = run_bass_kernel_spmd(nc, in_maps, core_ids=list(range(8)))
    return np.stack([res.results[b]["y"] for b in range(8)], axis=0)




# version 30
# speedup vs baseline: 1.0960x; 1.0213x over previous
import math
import numpy as np
import ml_dtypes
from contextlib import ExitStack
import concourse.bass as bass
import concourse.mybir as mybir
from concourse.bass_utils import run_bass_kernel_spmd

F32 = mybir.dt.float32
BF16 = mybir.dt.bfloat16
FP8 = mybir.dt.float8e4
AF = mybir.ActivationFunctionType
ALU = mybir.AluOpType
AXX = mybir.AxisListType.X

S_LEN = 2048
D = 1024
NT = 16
ND = 8
MASK_NEG = -1.0e4


def build_nc(n_win=NT, do_phase2=True, last_parts=3, skip=()):
    nc = bass.Bass("TRN2", target_bir_lowering=False)

    x_d = nc.dram_tensor("x", [S_LEN, D], F32, kind="ExternalInput").ap()
    w_d = nc.dram_tensor("w", [2 * D, D], F32, kind="ExternalInput").ap()
    wf_d = nc.dram_tensor("wf", [D, D], F32, kind="ExternalInput").ap()
    bias_d = nc.dram_tensor("bias", [1, D], F32, kind="ExternalInput").ap()
    scl_d = nc.dram_tensor("scl", [1, 2], F32, kind="ExternalInput").ap()
    mask_d = nc.dram_tensor("mask", [128, 128], F32, kind="ExternalInput").ap()
    idn_d = nc.dram_tensor("idn", [128, 128], BF16, kind="ExternalInput").ap()
    y_d = nc.dram_tensor("y", [S_LEN, D], F32, kind="ExternalOutput").ap()
    wvbf_d = nc.dram_tensor("wvbf_scratch", [D, D], BF16).ap()
    wfbf_d = nc.dram_tensor("wfbf_scratch", [D, D], BF16).ap()

    sched = []

    def op(eng, fn):
        sched.append((eng, fn))

    cnt = {"pe": 0, "ve": 0, "ac": 0, "ds": 0, "vs": 0, "vt": 0, "vp": 0, "vf": 0,
           "xs0": 0, "xs1": 0, "xs2": 0, "xs3": 0, "wks": 0, "wvs": 0, "wfs": 0,
           "xr0": 0, "xr1": 0, "xr2": 0, "xr3": 0, "do0": 0, "do1": 0, "po": 0,
           "xfs": 0, "wts": 0, "fts": 0, "qs0": 0}

    ctx = ExitStack()
    with ctx:
        sem = {k: ctx.enter_context(nc.semaphore(k)) for k in cnt}

        def wait(eng, key, val):
            if val <= 0:
                return
            op(eng, lambda e, s=sem[key], v=val: e.wait_ge(s, v))

        def bump(key, k=1):
            cnt[key] += k
            return cnt[key]

        xfm = ctx.enter_context(nc.sbuf_tensor("xfm", [128, ND, S_LEN], BF16))
        ct = ctx.enter_context(nc.sbuf_tensor("ct", [128, ND, S_LEN], BF16))
        vtm = ctx.enter_context(nc.sbuf_tensor("vtm", [128, NT, D], BF16))
        wft = ctx.enter_context(nc.sbuf_tensor("wft", [128, ND, D], BF16))
        biasb = ctx.enter_context(nc.sbuf_tensor("biasb", [128, D], F32))
        maskd = ctx.enter_context(nc.sbuf_tensor("maskd", [128, 128], F32))
        idn = ctx.enter_context(nc.sbuf_tensor("idn_sb", [128, 128], BF16))
        ones = ctx.enter_context(nc.sbuf_tensor("ones", [1, 128], F32))
        bias_sb = ctx.enter_context(nc.sbuf_tensor("bias_sb", [1, D], F32))
        scl_sb = ctx.enter_context(nc.sbuf_tensor("scl_sb", [1, 2], F32))
        sclb = ctx.enter_context(nc.sbuf_tensor("sclb", [128, 2], F32))
        invh = ctx.enter_context(nc.sbuf_tensor("invh", [128, 1], F32))
        ssb = ctx.enter_context(nc.sbuf_tensor("ssb", [128, 1], F32))
        cst = ctx.enter_context(nc.sbuf_tensor("cst", [128, 3], F32))
        stats = ctx.enter_context(nc.sbuf_tensor("stats", [128, 30], F32))
        pmax = ctx.enter_context(nc.sbuf_tensor("pmax", [128, 8], F32))

        def stat(k, par):
            return stats[:, 2 * k + par: 2 * k + par + 1]
        NMAX, LSUM, RECI, ST1, NM1, VS1, U1, RS1, ST2, NM2, VS2, RS2, ST1B, ST2B = range(14)
        LSUMB = 13
        LSUMB = 14

        op("sync", lambda e: e.dma_start(out=bias_sb[:], in_=bias_d).then_inc(sem["ds"], 16))
        op("sync", lambda e: e.dma_start(out=scl_sb[:], in_=scl_d).then_inc(sem["ds"], 16))
        op("sync", lambda e: e.dma_start(out=maskd[:], in_=mask_d).then_inc(sem["ds"], 16))
        op("sync", lambda e: e.dma_start(out=idn[:], in_=idn_d).then_inc(sem["ds"], 16))
        ds_consts = 64

        op("vector", lambda e: e.memset(cst[:, 0:1], 1e-5))
        op("vector", lambda e: e.memset(cst[:, 1:2], 1.0 / 1024.0))
        op("vector", lambda e: e.memset(cst[:, 2:3], -0.5))
        op("vector", lambda e: e.memset(ones[:, :], 1.0).then_inc(sem["ve"], 1))
        ve_init = bump("ve")

        def _preload(e):
            try:
                from concourse.hw_specs import get_activation_tables
                need = {AF.Exp, AF.Ln, AF.Copy, AF.Square}
                for idx, funcs in enumerate(get_activation_tables(e.bass.m.arch).values()):
                    if need.issubset(funcs):
                        e.add_instruction(mybir.InstLoadActFuncSet(
                            name=e.bass.get_next_instruction_name(),
                            act_func_set_id=idx, ins=[], outs=[]))
                        return
            except Exception:
                pass
        op("scalar", _preload)

        ctx1 = ExitStack()
        with ctx1:
            wbufA = ctx1.enter_context(nc.sbuf_tensor("wbufA", [128, ND, D], BF16))
            wvt = ctx1.enter_context(nc.sbuf_tensor("wvt", [128, ND, D], BF16))
            xtm = ctx1.enter_context(nc.sbuf_tensor("xtm", [128, NT, D], BF16))
            trp = [ctx1.enter_context(nc.psum_tensor(f"trp{p}", [128, ND, 128], BF16)) for p in range(2)]
            cps = [ctx1.enter_context(nc.psum_tensor(f"cps{c}", [128, 512], F32)) for c in range(4)]

            xchunks = [(0, 1), (1, 4), (4, 10), (10, 16)]
            for c, (t0, t1) in enumerate(xchunks):
                def f(e, c=c, t0=t0, t1=t1):
                    e.dma_start(out=xtm[:, t0:t1, :],
                                in_=x_d[t0 * 128:t1 * 128, :].rearrange("(t p) d -> p t d", p=128)
                                ).then_inc(sem[f"xs{c}"], 16)
                op("gpsimd", f)
                bump(f"xs{c}", 16)
            op("gpsimd", lambda e: e.dma_start(out=wbufA[:, :, :],
                                               in_=w_d[0:D, :].rearrange("(t p) d -> p t d", p=128))
               .then_inc(sem["wks"], 16))
            bump("wks", 16)
            op("gpsimd", lambda e: e.dma_start(out=wvbf_d[:, :], in_=w_d[D:2 * D, :])
               .then_inc(sem["wvs"], 16))
            bump("wvs", 16)
            op("gpsimd", lambda e: e.dma_start(out=wfbf_d[:, :], in_=wf_d)
               .then_inc(sem["wfs"], 16))
            bump("wfs", 16)

            ve_xt = [0] * NT
            pe_xt = [0] * NT
            xwait = {t0: c for c, (t0, t1) in enumerate(xchunks)}
            for t in range(NT):
                if t in xwait:
                    wait("tensor", f"xs{xwait[t]}", 16)
                if t == 0:
                    wait("tensor", "ds", ds_consts)
                if t >= 2:
                    wait("tensor", "ve", ve_xt[t - 2])
                for d in range(ND):
                    def f(e, t=t, d=d, last=(d == ND - 1)):
                        ins = e.transpose(trp[t % 2][:, d, :], xtm[:, t, d * 128:(d + 1) * 128], idn[:])
                        if last:
                            ins.then_inc(sem["pe"], 1)
                    op("tensor", f)
                pe_xt[t] = bump("pe")
                wait("vector", "pe", pe_xt[t])
                op("vector", lambda e, t=t: e.tensor_copy(xfm[:, :, t * 128:(t + 1) * 128], trp[t % 2][:, :, :])
                   .then_inc(sem["ve"], 1))
                ve_xt[t] = bump("ve")

            wait("sync", "wvs", 16)
            for dt in range(ND):
                op("sync", lambda e, dt=dt: e.dma_start_transpose(
                    wvt[:, dt, :], wvbf_d[:, dt * 128:(dt + 1) * 128]).then_inc(sem["wts"], 16))
                bump("wts", 16)
            wait("sync", "wfs", 16)
            for it in range(ND):
                op("sync", lambda e, it=it: e.dma_start_transpose(
                    wft[:, it, :], wfbf_d[:, it * 128:(it + 1) * 128]).then_inc(sem["fts"], 16))
                bump("fts", 16)

            wait("tensor", "ds", ds_consts)
            wait("tensor", "ve", ve_init)
            op("tensor", lambda e: e.matmul(cps[0][:, 0:2], ones[:, :], scl_sb[:, :], start=True, stop=True))
            op("tensor", lambda e: e.matmul(cps[1][:, 0:512], ones[:, :], bias_sb[:, 0:512], start=True, stop=True))
            op("tensor", lambda e: e.matmul(cps[2][:, 0:512], ones[:, :], bias_sb[:, 512:1024], start=True, stop=True)
               .then_inc(sem["pe"], 1))
            pe_bcast = bump("pe")
            wait("vector", "pe", pe_bcast)
            op("vector", lambda e: e.tensor_copy(sclb[:, :], cps[0][:, 0:2]))
            op("vector", lambda e: e.drain())
            op("vector", lambda e: e.reciprocal(invh[:, :], sclb[:, 0:1]))
            op("vector", lambda e: e.drain())
            op("vector", lambda e: e.tensor_scalar_mul(invh[:, :], invh[:, :], 0.5))
            op("vector", lambda e: e.tensor_mul(ssb[:, :], sclb[:, 0:1], sclb[:, 1:2]))
            op("vector", lambda e: e.tensor_copy(biasb[:, 0:512], cps[1][:, 0:512]))
            op("vector", lambda e: e.tensor_copy(biasb[:, 512:1024], cps[2][:, 0:512])
               .then_inc(sem["ve"], 1))
            ve_bcast = bump("ve")

            wait("tensor", "ve", ve_xt[NT - 1])
            wait("tensor", "wks", 16)
            wait("tensor", "ve", ve_bcast)
            ve_ct_half = {}
            kglob = 0
            for dt in range(ND):
                for h in range(2):
                    pair = kglob % 2
                    if kglob >= 2:
                        wait("tensor", "ve", ve_ct_half[kglob - 2])
                    for j in range(ND):
                        for cc in range(2):
                            c = h * 2 + cc
                            bank = cps[2 * pair + cc]
                            def f(e, dt=dt, j=j, c=c, bank=bank, last=(j == ND - 1 and cc == 1)):
                                ins = e.matmul(bank[:, 0:512],
                                               wbufA[:, j, dt * 128:(dt + 1) * 128],
                                               xfm[:, j, c * 512:(c + 1) * 512],
                                               start=(j == 0), stop=(j == ND - 1))
                                if last:
                                    ins.then_inc(sem["pe"], 1)
                            op("tensor", f)
                    pe_h = bump("pe")
                    wait("vector", "pe", pe_h)
                    for cc in range(2):
                        c = h * 2 + cc
                        bank = cps[2 * pair + cc]
                        def f(e, dt=dt, c=c, bank=bank, last=(cc == 1)):
                            ins = e.tensor_copy(ct[:, dt, c * 512:(c + 1) * 512], bank[:, 0:512])
                            if last:
                                ins.then_inc(sem["ve"], 1)
                        op("vector", f)
                    ve_ct_half[kglob] = bump("ve")
                    kglob += 1
            pe_ct_done = cnt["pe"]
            ve_ct_done = cnt["ve"]

            wait("tensor", "wts", 128)
            ve_v = [0] * NT
            for st in range(NT):
                pair = st % 2
                if st >= 2:
                    wait("tensor", "ve", ve_v[st - 2])
                elif st == 0:
                    wait("tensor", "ve", ve_ct_done)
                for d in range(ND):
                    for cc in range(2):
                        def f(e, st=st, d=d, cc=cc, pair=pair, last=(d == ND - 1 and cc == 1)):
                            ins = e.matmul(cps[2 * pair + cc][:, 0:512],
                                           xfm[:, d, st * 128:(st + 1) * 128],
                                           wvt[:, d, cc * 512:(cc + 1) * 512],
                                           start=(d == 0), stop=(d == ND - 1))
                            if last:
                                ins.then_inc(sem["pe"], 1)
                        op("tensor", f)
                pe_v = bump("pe")
                wait("vector", "pe", pe_v)
                for cc in range(2):
                    def f(e, st=st, cc=cc, pair=pair, last=(cc == 1)):
                        ins = e.tensor_copy(vtm[:, st, cc * 512:(cc + 1) * 512], cps[2 * pair + cc][:, 0:512])
                        if last:
                            ins.then_inc(sem["ve"], 1)
                    op("vector", f)
                ve_v[st] = bump("ve")

            wait("tensor", "fts", 128)
            ve_phase1_done = cnt["ve"]
            pe_phase1_done = cnt["pe"]

        ctx2 = ExitStack()
        with ctx2:
            sps = [ctx2.enter_context(nc.psum_tensor(f"sps{c}", [128, 512], F32)) for c in range(4)]
            pvps = [ctx2.enter_context(nc.psum_tensor(f"pvps{c}", [128, 512], F32)) for c in range(2)]
            trp2 = [ctx2.enter_context(nc.psum_tensor(f"trp2_{p}", [128, 8, 128], BF16)) for p in range(2)]

            s_sb = [ctx2.enter_context(nc.sbuf_tensor(f"s_sb{p}", [128, S_LEN], F32)) for p in range(2)]
            p_bf = [ctx2.enter_context(nc.sbuf_tensor(f"p_bf{p}", [128, S_LEN], BF16)) for p in range(2)]
            pt = ctx2.enter_context(nc.sbuf_tensor("pt", [128, NT, 128], BF16))
            xr = [ctx2.enter_context(nc.sbuf_tensor(f"xr{p}", [128, D], F32)) for p in range(2)]
            t_sb = [ctx2.enter_context(nc.sbuf_tensor(f"t_sb{p}", [128, D], F32)) for p in range(2)]
            y1 = [ctx2.enter_context(nc.sbuf_tensor(f"y1_{p}", [128, D], F32)) for p in range(2)]
            sq = [ctx2.enter_context(nc.sbuf_tensor(f"sq{p}", [128, D], F32)) for p in range(2)]
            q8 = [ctx2.enter_context(nc.sbuf_tensor("q8_0", [128, D], FP8))] * 2
            yqb2 = [ctx2.enter_context(nc.sbuf_tensor("yqb2_0", [128, D], BF16))] * 2
            yqb = [ctx2.enter_context(nc.sbuf_tensor(f"yqb{p}", [128, D], BF16)) for p in range(2)]
            yqt = ctx2.enter_context(nc.sbuf_tensor("yqt", [128, ND, 128], BF16))
            t2 = [ctx2.enter_context(nc.sbuf_tensor(f"t2_{p}", [128, D], F32)) for p in range(2)]
            y2 = [ctx2.enter_context(nc.sbuf_tensor(f"y2_{p}", [128, D], F32)) for p in range(2)]



            dg_xr = [0] * NT
            ve_pvevac = [0] * NT
            ac_scopy = [0] * NT
            ve_diag = [0] * NT
            ve_max = [0] * NT
            ac_exp = [0] * NT
            ve_tevac = [0] * NT
            pe_s = [0] * NT
            pe_pv = [0] * NT
            pe_yt = [0] * NT
            pe_ffn = [0] * NT
            ve_sevac_done = [0] * NT
            ve_ytevac = [0] * NT
            ve_ffnevac = [0] * NT
            ve_y2 = [0] * NT
            ve_q = [0] * NT
            ac_rstd1 = [0] * NT
            ac_rstd2 = [0] * NT
            do_store = [0] * NT
            ve_nm1 = [0] * NT
            ve_pvstt_c = [[0, 0] for _ in range(NT)]
            pe_ffn_c = [[0, 0] for _ in range(NT)]
            ve_scopyd = [0] * NT
            tev_groups = [None] * NT
            pe_s_chunk = [None] * NT
            ac_scopy_c = [[0] * 4 for _ in range(NT)]
            pe_yt_a = [0] * NT
            ve_nm2 = [0] * NT
            ac_exp_a = [0] * NT
            exp_split = [False] * NT
            ve_ytev_a = [0] * NT
            ve_y1 = [0] * NT
            yt_tb = [0] * NT
            trp2_free2 = [("ac", 0), ("ac", 0)]
            tb_ctr = 0

            def emit_xr(i):
                nonlocal dg_xr
                k = f"xr{i % 4}"
                if i >= 2:
                    wait("gpsimd", "vp", ve_pvevac[i - 2])
                op("gpsimd", lambda e, i=i, k=k: e.dma_start(out=xr[i % 2][:, :], in_=x_d[i * 128:(i + 1) * 128, :])
                   .then_inc(sem[k], 16))
                v1 = bump(k, 16)
                wait("gpsimd", k, v1)
                op("gpsimd", lambda e, i=i, k=k: e.dma_start(out=xr[i % 2][:, :], in_=x_d[i * 128:(i + 1) * 128, :],
                                                             accum_op=ALU.add).then_inc(sem[k], 16))
                dg_xr[i] = bump(k, 16)

            def emit_S(i):
                par = i % 2
                nch = (i + 1) * 128 // 512 + (1 if ((i + 1) * 128) % 512 else 0)
                if i == 0:
                    wait("tensor", "ve", ve_phase1_done)
                nch_prev = (i * 128 + 511) // 512
                pe_s_chunk[i] = [0] * nch
                for c in range(nch):
                    if 1 <= i and c < nch_prev:
                        wait("tensor", "ac", ac_scopy_c[i - 1][c])
                    w = min(512, (i + 1) * 128 - c * 512)
                    for j in range(ND):
                        def f(e, i=i, j=j, c=c, w=w, last=(j == ND - 1)):
                            ins = e.matmul(sps[c][:, 0:w],
                                           ct[:, j, i * 128:(i + 1) * 128],
                                           xfm[:, j, c * 512:c * 512 + w],
                                           start=(j == 0), stop=(j == ND - 1))
                            if last:
                                ins.then_inc(sem["pe"], 1)
                        op("tensor", f)
                    pe_s_chunk[i][c] = bump("pe")
                pe_s[i] = pe_s_chunk[i][-1]

            def emit_scopy(i):
                par = i % 2
                W = (i + 1) * 128
                nch = W // 512 + (1 if W % 512 else 0)
                if i >= 2:
                    wait("scalar", "vs", ve_max[i - 2])
                for c in range(nch):
                    w = min(512, W - c * 512)
                    wait("scalar", "pe", pe_s_chunk[i][c])
                    def f(e, c=c, w=w, par=par):
                        e.copy(s_sb[par][:, c * 512:c * 512 + w], sps[c][:, 0:w]).then_inc(sem["ac"], 1)
                    op("scalar", f)
                    ac_scopy_c[i][c] = bump("ac")
                ac_scopy[i] = ac_scopy_c[i][nch - 1]
                ve_scopyd[i] = 0

            def emit_diag(i):
                par = i % 2
                W = (i + 1) * 128
                nch = W // 512 + (1 if W % 512 else 0)
                dcol = i * 128
                if i >= 2:
                    wait("vector", "ac", ac_exp[i - 2])
                for c in range(nch - 1):
                    wait("vector", "ac", ac_scopy_c[i][c])
                    def f(e, c=c, par=par):
                        e.reduce_max(pmax[:, c:c + 1], s_sb[par][:, c * 512:(c + 1) * 512],
                                     axis=AXX, negate=True)
                    op("vector", f)
                wait("vector", "ac", ac_scopy_c[i][nch - 1])
                op("vector", lambda e: e.drain())
                op("vector", lambda e, par=par, dcol=dcol: e.tensor_add(
                    s_sb[par][:, dcol:dcol + 128], s_sb[par][:, dcol:dcol + 128], maskd[:])
                    .then_inc(sem["vs"], 1))
                ve_diag[i] = bump("vs")

            def emit_max(i):
                par = i % 2
                W = (i + 1) * 128
                nch = W // 512 + (1 if W % 512 else 0)
                w = W - (nch - 1) * 512
                op("vector", lambda e: e.drain())
                def f(e, c=nch - 1, w=w, par=par):
                    e.reduce_max(pmax[:, c:c + 1], s_sb[par][:, c * 512:c * 512 + w],
                                 axis=AXX, negate=True)
                op("vector", f)
                op("vector", lambda e: e.drain())
                op("vector", lambda e, par=par, nch=nch: e.tensor_reduce(
                    stat(NMAX, par), pmax[:, 0:nch], AXX, ALU.min).then_inc(sem["vs"], 1))
                ve_max[i] = bump("vs")
            def emit_exp(i):
                par = i % 2
                W = (i + 1) * 128
                Wa = min(W, 1024)
                wait("scalar", "vs", ve_max[i])
                if i >= 2:
                    wait("scalar", "pe", pe_t_last[i - 2])
                op("scalar", lambda e, par=par, Wa=Wa: e.activation(
                    p_bf[par][:, 0:Wa], s_sb[par][:, 0:Wa], AF.Exp,
                    bias=stat(NMAX, par), accum_out=stat(LSUM, par)).then_inc(sem["ac"], 1))
                ac_exp_a[i] = bump("ac")
                if W > 1024:
                    op("scalar", lambda e: e.drain())
                    op("scalar", lambda e, par=par, W=W: e.activation(
                        p_bf[par][:, 1024:W], s_sb[par][:, 1024:W], AF.Exp,
                        bias=stat(NMAX, par), accum_out=stat(LSUMB, par)).then_inc(sem["ac"], 1))
                    ac_exp[i] = bump("ac")
                else:
                    ac_exp[i] = ac_exp_a[i]
                exp_split[i] = (W > 1024)

            def emit_recip(i):
                par = i % 2
                wait("vector", "ac", ac_exp[i])
                if exp_split[i]:
                    op("vector", lambda e, par=par: e.tensor_add(stat(LSUM, par), stat(LSUM, par),
                                                                 stat(LSUMB, par)))
                    op("vector", lambda e: e.drain())
                op("vector", lambda e, par=par: e.reciprocal(stat(RECI, par), stat(LSUM, par))
                   .then_inc(sem["vs"], 1))
                op("vector", lambda e: e.drain())
                bump("vs")

            pe_t_last = [0] * NT

            def emit_T(i):
                nonlocal tb_ctr
                ntile = i + 1
                par = i % 2
                col = 0
                evs = []
                while col < ntile:
                    ng = min(4, ntile - col)
                    tb = tb_ctr % 2
                    tb_ctr += 1
                    wait("tensor", trp2_free2[tb][0], trp2_free2[tb][1])
                    if col < 8:
                        wait("tensor", "ac", ac_exp_a[i])
                    else:
                        wait("tensor", "ac", ac_exp[i])
                    for u in range(ng):
                        def f(e, par=par, col=col, u=u, tb=tb, last=(u == ng - 1)):
                            ins = e.transpose(trp2[tb][:, u, :],
                                              p_bf[par][:, (col + u) * 128:(col + u + 1) * 128], idn[:])
                            if last:
                                ins.then_inc(sem["pe"], 1)
                        op("tensor", f)
                    pe_g = bump("pe")
                    wait("scalar", "pe", pe_g)
                    op("scalar", lambda e, col=col, ng=ng, tb=tb: e.copy(
                        pt[:, col:col + ng, :], trp2[tb][:, 0:ng, :]).then_inc(sem["ac"], 1))
                    trp2_free2[tb] = ("ac", bump("ac"))
                    evs.append(trp2_free2[tb][1])
                    col += ng
                pe_t_last[i] = pe_g
                tev_groups[i] = list(evs)
                ve_tevac[i] = evs[-1]

            pv_free_ve = [("vp", 0)]

            def emit_PV(i):
                par = i % 2
                wait("tensor", pv_free_ve[0][0], pv_free_ve[0][1])
                for t in range(i + 1):
                    if t % 4 == 0:
                        wait("tensor", "ac", tev_groups[i][t // 4])
                    for cc in range(2):
                        def f(e, t=t, cc=cc, i=i, last=(t == i and cc == 1)):
                            ins = e.matmul(pvps[cc][:, 0:512], pt[:, t, :],
                                           vtm[:, t, cc * 512:(cc + 1) * 512],
                                           start=(t == 0), stop=(t == i))
                            if last:
                                ins.then_inc(sem["pe"], 1)
                        op("tensor", f)
                pe_pv[i] = bump("pe")

            def emit_pvstt(i):
                par = i % 2
                wait("vector", "pe", pe_pv[i])
                wait("vector", f"xr{i % 4}", dg_xr[i])
                for cc in range(2):
                    def f(e, cc=cc, par=par):
                        e.scalar_tensor_tensor(t_sb[par][:, cc * 512:(cc + 1) * 512],
                                               pvps[cc][:, 0:512], stat(RECI, par),
                                               xr[par][:, cc * 512:(cc + 1) * 512],
                                               ALU.mult, ALU.add,
                                               accum_out=stat(ST1 if cc == 0 else ST1B, par)
                                               ).then_inc(sem["vp"], 1)
                    op("vector", f)
                    ve_pvstt_c[i][cc] = bump("vp")
                ve_pvevac[i] = ve_pvstt_c[i][1]
                pv_free_ve[0] = ("vp", ve_pvevac[i])
                op("vector", lambda e: e.drain())
                op("vector", lambda e, par=par: e.tensor_scalar(stat(NM1, par), stat(ST1, par),
                                                                stat(ST1B, par), -1.0 / 1024.0,
                                                                ALU.add, ALU.mult).then_inc(sem["vp"], 1))
                ve_nm1[i] = bump("vp")

            def emit_ln1(i):
                par = i % 2
                wait("scalar", "vp", ve_nm1[i])
                op("scalar", lambda e, par=par: e.activation(sq[par][:, :], t_sb[par][:, :], AF.Square,
                                                             bias=stat(NM1, par), accum_out=stat(VS1, par)))
                op("scalar", lambda e: e.drain())
                op("scalar", lambda e, par=par: e.activation(stat(U1, par), stat(VS1, par), AF.Ln,
                                                             scale=cst[:, 1:2], bias=cst[:, 0:1]))
                op("scalar", lambda e: e.drain())
                op("scalar", lambda e, par=par: e.activation(stat(RS1, par), stat(U1, par), AF.Exp,
                                                             scale=cst[:, 2:3]).then_inc(sem["ac"], 1))
                ac_rstd1[i] = bump("ac")

            def emit_y1q(i):
                par = i % 2
                wait("vector", "ac", ac_rstd1[i])
                op("vector", lambda e, par=par: e.tensor_scalar(y1[par][:, :], t_sb[par][:, :],
                                                                stat(NM1, par), stat(RS1, par),
                                                                ALU.add, ALU.mult).then_inc(sem["vp"], 1))
                ve_y1[i] = bump("vp")
                op("vector", lambda e: e.drain())
                op("vector", lambda e, par=par: e.tensor_scalar(sq[par][:, :], y1[par][:, :],
                                                                invh[:, 0:1], 224.0, ALU.mult, ALU.min))
                op("vector", lambda e: e.drain())
                op("vector", lambda e, par=par: e.tensor_scalar(q8[par][:, :], sq[par][:, :],
                                                                -224.0, None, ALU.max))
                op("vector", lambda e: e.drain())
                if i >= 2:
                    wait("vector", "qs0", 16 * (i - 1))
                op("vector", lambda e, par=par: e.tensor_scalar(yqb[par][:, :], q8[par][:, :],
                                                                2.0, None, ALU.mult).then_inc(sem["vp"], 1))
                ve_q[i] = bump("vp")
                wait("sync", "vp", ve_q[i])
                if i >= 1:
                    wait("sync", "pe", pe_yt[i - 1])
                op("sync", lambda e, par=par: e.dma_start(
                    out=yqb2[par][:, :].bitcast(mybir.dt.int16),
                    in_=yqb[par][:, :].bitcast(mybir.dt.int16)).then_inc(sem["qs0"], 16))
                bump("qs0", 16)

            def emit_yT(i):
                nonlocal tb_ctr
                par = i % 2
                tb1 = tb_ctr % 2
                tb_ctr += 1
                tb2 = tb_ctr % 2
                tb_ctr += 1
                yt_tb[i] = (tb1, tb2)
                wait("tensor", "qs0", 16 * (i + 1))
                wait("tensor", trp2_free2[tb1][0], trp2_free2[tb1][1])
                for u in range(4):
                    def f(e, par=par, u=u, tb=tb1, last=(u == 3)):
                        ins = e.transpose(trp2[tb][:, u, :], yqb2[par][:, u * 128:(u + 1) * 128], idn[:])
                        if last:
                            ins.then_inc(sem["pe"], 1)
                    op("tensor", f)
                pe_yt_a[i] = bump("pe")
                wait("tensor", trp2_free2[tb2][0], trp2_free2[tb2][1])
                for u in range(4, 8):
                    def f(e, par=par, u=u, tb=tb2, last=(u == 7)):
                        ins = e.transpose(trp2[tb][:, u - 4, :], yqb2[par][:, u * 128:(u + 1) * 128], idn[:])
                        if last:
                            ins.then_inc(sem["pe"], 1)
                    op("tensor", f)
                pe_yt[i] = bump("pe")

            def emit_ytevac(i):
                tb1, tb2 = yt_tb[i]
                wait("scalar", "pe", pe_yt_a[i])
                op("scalar", lambda e, tb=tb1: e.copy(yqt[:, 0:4, :], trp2[tb][:, 0:4, :])
                   .then_inc(sem["ac"], 1))
                trp2_free2[tb1] = ("ac", bump("ac"))
                ve_ytev_a[i] = trp2_free2[tb1][1]
                wait("scalar", "pe", pe_yt[i])
                op("scalar", lambda e, tb=tb2: e.copy(yqt[:, 4:8, :], trp2[tb][:, 0:4, :])
                   .then_inc(sem["ac"], 1))
                trp2_free2[tb2] = ("ac", bump("ac"))
                ve_ytevac[i] = trp2_free2[tb2][1]

            def emit_FFN(i):
                wait("tensor", "ac", ve_ytev_a[i])
                pe_ffn_c[i] = [0, 0]
                for cc in range(2):
                    wait("tensor", "vp", ve_pvstt_c[min(i + 1, NT - 1)][cc])
                    if cc == 0:
                        pass
                    else:
                        wait("tensor", "ac", ve_ytevac[i])
                    for ic in range(ND):
                        if cc == 0 and ic == 4:
                            wait("tensor", "ac", ve_ytevac[i])
                        def f(e, ic=ic, cc=cc, last=(ic == ND - 1)):
                            ins = e.matmul(pvps[cc][:, 0:512], yqt[:, ic, :],
                                           wft[:, ic, cc * 512:(cc + 1) * 512],
                                           start=(ic == 0), stop=(ic == ND - 1))
                            if last:
                                ins.then_inc(sem["pe"], 1)
                        op("tensor", f)
                    pe_ffn_c[i][cc] = bump("pe")
                pe_ffn[i] = pe_ffn_c[i][1]

            def emit_ffnstt(i):
                par = i % 2
                op("vector", lambda e, par=par: e.tensor_add(sq[par][:, :], y1[par][:, :], biasb[:, :]))
                op("vector", lambda e: e.drain())
                if i >= 2:
                    wait("vector", f"do{i % 2}", do_store[i - 2])
                for cc in range(2):
                    wait("vector", "pe", pe_ffn_c[i][cc])
                    def f(e, cc=cc, par=par, last=(cc == 1)):
                        ins = e.scalar_tensor_tensor(t2[par][:, cc * 512:(cc + 1) * 512],
                                                     pvps[cc][:, 0:512], ssb[:, 0:1],
                                                     sq[par][:, cc * 512:(cc + 1) * 512],
                                                     ALU.mult, ALU.add,
                                                     accum_out=stat(ST2 if cc == 0 else ST2B, par))
                        if last:
                            ins.then_inc(sem["vf"], 1)
                    op("vector", f)
                ve_ffnevac[i] = bump("vf")
                pv_free_ve[0] = ("vf", ve_ffnevac[i])
                op("vector", lambda e: e.drain())
                op("vector", lambda e, par=par: e.tensor_scalar(stat(NM2, par), stat(ST2, par),
                                                                stat(ST2B, par), -1.0 / 1024.0,
                                                                ALU.add, ALU.mult).then_inc(sem["vf"], 1))
                ve_nm2[i] = bump("vf")

            def emit_ln2y2(i):
                par = i % 2
                wait("scalar", "vf", ve_nm2[i])
                op("scalar", lambda e, par=par: e.activation(sq[par][:, :], t2[par][:, :], AF.Square,
                                                             bias=stat(NM2, par), accum_out=stat(VS2, par)))
                op("scalar", lambda e: e.drain())
                op("scalar", lambda e, par=par: e.activation(stat(U1, par), stat(VS2, par), AF.Ln,
                                                             scale=cst[:, 1:2], bias=cst[:, 0:1]))
                op("scalar", lambda e: e.drain())
                op("scalar", lambda e, par=par: e.activation(stat(RS2, par), stat(U1, par), AF.Exp,
                                                             scale=cst[:, 2:3]).then_inc(sem["ac"], 1))
                ac_rstd2[i] = bump("ac")
                wait("vector", "ac", ac_rstd2[i])
                op("vector", lambda e, par=par: e.tensor_scalar(y2[par][:, :], t2[par][:, :],
                                                                stat(NM2, par), stat(RS2, par),
                                                                ALU.add, ALU.mult).then_inc(sem["vf"], 1))
                ve_y2[i] = bump("vf")
                wait("sync", "vf", ve_y2[i])
                op("sync", lambda e, i=i, par=par: e.dma_start(out=y_d[i * 128:(i + 1) * 128, :], in_=y2[par][:, :])
                   .then_inc(sem[f"do{par}"], 16))
                do_store[i] = bump(f"do{par}", 16)

            if do_phase2 and n_win > 0:
                wait("gpsimd", "pe", pe_phase1_done)
                wait("gpsimd", "ve", ve_phase1_done)
                emit_xr(0)
                if n_win > 1:
                    emit_xr(1)
                emit_S(0)
                emit_scopy(0)
                emit_diag(0)
                emit_max(0)
                emit_exp(0)
                emit_recip(0)
                emit_T(0)
                for i in range(n_win):
                    if i + 1 < n_win:
                        emit_S(i + 1)
                        emit_scopy(i + 1)
                        emit_diag(i + 1)
                        emit_max(i + 1)
                        emit_exp(i + 1)
                    emit_PV(i)
                    emit_pvstt(i)
                    if i >= 1:
                        emit_yT(i - 1)
                        emit_ytevac(i - 1)
                    emit_ln1(i)
                    if i + 2 < n_win:
                        emit_xr(i + 2)
                    if i >= 1:
                        emit_FFN(i - 1)
                        emit_ffnstt(i - 1)
                    emit_y1q(i)
                    if i + 1 < n_win:
                        emit_recip(i + 1)
                        emit_T(i + 1)
                    if i >= 1:
                        emit_ln2y2(i - 1)
                emit_yT(n_win - 1)
                emit_ytevac(n_win - 1)
                emit_FFN(n_win - 1)
                emit_ffnstt(n_win - 1)
                emit_ln2y2(n_win - 1)
                wait("sync", "do0", cnt["do0"])
                wait("sync", "do1", cnt["do1"])
            else:
                wait("vector", "ve", ve_phase1_done)
                op("vector", lambda e: e.tensor_copy(t2[0][:, :], ct[:, 0, 0:1024]).then_inc(sem["ve"], 1))
                vend = bump("ve")
                wait("sync", "ve", vend)
                op("sync", lambda e: e.dma_start(out=y_d[0:128, :], in_=t2[0][:, :]).then_inc(sem["do0"], 16))
                bump("do0", 16)
                wait("sync", "do0", cnt["do0"])

        block = ctx.enter_context(nc.Block())

        def runner(name):
            def run(eng):
                for e, fn in sched:
                    if e == name:
                        fn(eng)
            return run
        block.sync(runner("sync"))
        block.gpsimd(runner("gpsimd"))
        block.tensor(runner("tensor"))
        block.vector(runner("vector"))
        block.scalar(runner("scalar"))

    return nc


_NC_CACHE = [None]


def kernel(L0_attn_x, L0_attn_W, L0_ffn_weight_fp8, L0_ffn_bias, L0_ffn_input_scale, L0_ffn_weight_scale):
    x = np.ascontiguousarray(np.asarray(L0_attn_x, dtype=np.float32))
    W = np.ascontiguousarray(np.asarray(L0_attn_W, dtype=np.float32))
    Wf = np.ascontiguousarray(np.asarray(L0_ffn_weight_fp8, dtype=np.float32))
    bias = np.ascontiguousarray(np.asarray(L0_ffn_bias, dtype=np.float32).reshape(1, D))
    scl = np.ascontiguousarray(np.stack([np.asarray(L0_ffn_input_scale, dtype=np.float32).reshape(()),
                                         np.asarray(L0_ffn_weight_scale, dtype=np.float32).reshape(())]).reshape(1, 2))
    r = np.arange(128)
    mask = np.where(r[None, :] <= r[:, None], np.float32(0.0), np.float32(MASK_NEG)).astype(np.float32)
    idn = np.eye(128, dtype=ml_dtypes.bfloat16)

    if _NC_CACHE[0] is None:
        _NC_CACHE[0] = build_nc()
    nc = _NC_CACHE[0]
    in_maps = [{"x": x[b], "w": W, "wf": Wf, "bias": bias, "scl": scl, "mask": mask, "idn": idn}
               for b in range(8)]
    res = run_bass_kernel_spmd(nc, in_maps, core_ids=list(range(8)))
    return np.stack([res.results[b]["y"] for b in range(8)], axis=0)




# version 32
# speedup vs baseline: 1.1020x; 1.0054x over previous
import math
import numpy as np
import ml_dtypes
from contextlib import ExitStack
import concourse.bass as bass
import concourse.mybir as mybir
from concourse.bass_utils import run_bass_kernel_spmd

F32 = mybir.dt.float32
BF16 = mybir.dt.bfloat16
FP8 = mybir.dt.float8e4
AF = mybir.ActivationFunctionType
ALU = mybir.AluOpType
AXX = mybir.AxisListType.X

S_LEN = 2048
D = 1024
NT = 16
ND = 8
MASK_NEG = -1.0e4


def build_nc(n_win=NT, do_phase2=True, last_parts=3, skip=()):
    nc = bass.Bass("TRN2", target_bir_lowering=False)

    x_d = nc.dram_tensor("x", [S_LEN, D], F32, kind="ExternalInput").ap()
    w_d = nc.dram_tensor("w", [2 * D, D], F32, kind="ExternalInput").ap()
    wf_d = nc.dram_tensor("wf", [D, D], F32, kind="ExternalInput").ap()
    bias_d = nc.dram_tensor("bias", [1, D], F32, kind="ExternalInput").ap()
    scl_d = nc.dram_tensor("scl", [1, 2], F32, kind="ExternalInput").ap()
    mask_d = nc.dram_tensor("mask", [128, 128], F32, kind="ExternalInput").ap()
    idn_d = nc.dram_tensor("idn", [128, 128], BF16, kind="ExternalInput").ap()
    y_d = nc.dram_tensor("y", [S_LEN, D], F32, kind="ExternalOutput").ap()
    wvbf_d = nc.dram_tensor("wvbf_scratch", [D, D], BF16).ap()
    wfbf_d = nc.dram_tensor("wfbf_scratch", [D, D], BF16).ap()

    sched = []

    def op(eng, fn):
        sched.append((eng, fn))

    cnt = {"pe": 0, "ve": 0, "ac": 0, "ds": 0, "vs": 0, "vt": 0, "vp": 0, "vf": 0,
           "xs0": 0, "xs1": 0, "xs2": 0, "xs3": 0, "wks": 0, "wvs": 0, "wfs": 0,
           "xr0": 0, "xr1": 0, "xr2": 0, "xr3": 0, "do0": 0, "do1": 0, "po": 0,
           "xfs": 0, "wts": 0, "fts": 0, "qs0": 0}

    ctx = ExitStack()
    with ctx:
        sem = {k: ctx.enter_context(nc.semaphore(k)) for k in cnt}

        def wait(eng, key, val):
            if val <= 0:
                return
            op(eng, lambda e, s=sem[key], v=val: e.wait_ge(s, v))

        def bump(key, k=1):
            cnt[key] += k
            return cnt[key]

        xfm = ctx.enter_context(nc.sbuf_tensor("xfm", [128, ND, S_LEN], BF16))
        ct = ctx.enter_context(nc.sbuf_tensor("ct", [128, ND, S_LEN], BF16))
        vtm = ctx.enter_context(nc.sbuf_tensor("vtm", [128, NT, D], BF16))
        wft = ctx.enter_context(nc.sbuf_tensor("wft", [128, ND, D], BF16))
        biasb = ctx.enter_context(nc.sbuf_tensor("biasb", [128, D], F32))
        maskd = ctx.enter_context(nc.sbuf_tensor("maskd", [128, 128], F32))
        idn = ctx.enter_context(nc.sbuf_tensor("idn_sb", [128, 128], BF16))
        ones = ctx.enter_context(nc.sbuf_tensor("ones", [1, 128], F32))
        bias_sb = ctx.enter_context(nc.sbuf_tensor("bias_sb", [1, D], F32))
        scl_sb = ctx.enter_context(nc.sbuf_tensor("scl_sb", [1, 2], F32))
        sclb = ctx.enter_context(nc.sbuf_tensor("sclb", [128, 2], F32))
        invh = ctx.enter_context(nc.sbuf_tensor("invh", [128, 1], F32))
        ssb = ctx.enter_context(nc.sbuf_tensor("ssb", [128, 1], F32))
        cst = ctx.enter_context(nc.sbuf_tensor("cst", [128, 3], F32))
        stats = ctx.enter_context(nc.sbuf_tensor("stats", [128, 30], F32))
        pmax = ctx.enter_context(nc.sbuf_tensor("pmax", [128, 8], F32))

        def stat(k, par):
            return stats[:, 2 * k + par: 2 * k + par + 1]
        NMAX, LSUM, RECI, ST1, NM1, VS1, U1, RS1, ST2, NM2, VS2, RS2, ST1B, ST2B = range(14)
        LSUMB = 13
        LSUMB = 14

        op("sync", lambda e: e.dma_start(out=idn[:], in_=idn_d).then_inc(sem["ds"], 16))
        op("sync", lambda e: e.dma_start(out=bias_sb[:], in_=bias_d).then_inc(sem["ds"], 16))
        op("sync", lambda e: e.dma_start(out=scl_sb[:], in_=scl_d).then_inc(sem["ds"], 16))
        op("sync", lambda e: e.dma_start(out=maskd[:], in_=mask_d).then_inc(sem["ds"], 16))
        ds_idn = 16
        ds_bcast = 48
        ds_all = 64

        op("vector", lambda e: e.memset(cst[:, 0:1], 1e-5))
        op("vector", lambda e: e.memset(cst[:, 1:2], 1.0 / 1024.0))
        op("vector", lambda e: e.memset(cst[:, 2:3], -0.5))
        op("vector", lambda e: e.memset(ones[:, :], 1.0).then_inc(sem["ve"], 1))
        ve_init = bump("ve")

        def _preload(e):
            try:
                from concourse.hw_specs import get_activation_tables
                need = {AF.Exp, AF.Ln, AF.Copy, AF.Square}
                for idx, funcs in enumerate(get_activation_tables(e.bass.m.arch).values()):
                    if need.issubset(funcs):
                        e.add_instruction(mybir.InstLoadActFuncSet(
                            name=e.bass.get_next_instruction_name(),
                            act_func_set_id=idx, ins=[], outs=[]))
                        return
            except Exception:
                pass
        op("scalar", _preload)

        ctx1 = ExitStack()
        with ctx1:
            wbufA = ctx1.enter_context(nc.sbuf_tensor("wbufA", [128, ND, D], BF16))
            wvt = ctx1.enter_context(nc.sbuf_tensor("wvt", [128, ND, D], BF16))
            xtm = ctx1.enter_context(nc.sbuf_tensor("xtm", [128, NT, D], BF16))
            trp = [ctx1.enter_context(nc.psum_tensor(f"trp{p}", [128, ND, 128], BF16)) for p in range(2)]
            cps = [ctx1.enter_context(nc.psum_tensor(f"cps{c}", [128, 512], F32)) for c in range(4)]

            op("gpsimd", lambda e: e.dma_start(
                out=xtm[:, 0:1, 0:512],
                in_=x_d[0:128, 0:512].rearrange("(t p) d -> p t d", p=128)).then_inc(sem["xs0"], 16))
            bump("xs0", 16)
            op("gpsimd", lambda e: e.dma_start(
                out=xtm[:, 0:1, 512:1024],
                in_=x_d[0:128, 512:1024].rearrange("(t p) d -> p t d", p=128)).then_inc(sem["xs0"], 16))
            bump("xs0", 16)
            for (t0, t1, k) in [(1, 4, "xs1"), (4, 8, "xs2")]:
                def f(e, t0=t0, t1=t1, k=k):
                    e.dma_start(out=xtm[:, t0:t1, :],
                                in_=x_d[t0 * 128:t1 * 128, :].rearrange("(t p) d -> p t d", p=128)
                                ).then_inc(sem[k], 16)
                op("gpsimd", f)
                bump(k, 16)
            for h in range(2):
                op("gpsimd", lambda e, h=h: e.dma_start(
                    out=wbufA[:, :, h * 512:(h + 1) * 512],
                    in_=w_d[0:D, h * 512:(h + 1) * 512].rearrange("(t p) d -> p t d", p=128))
                   .then_inc(sem["wks"], 16))
                bump("wks", 16)
            op("gpsimd", lambda e: e.dma_start(out=xtm[:, 8:16, :],
                                               in_=x_d[8 * 128:16 * 128, :].rearrange("(t p) d -> p t d", p=128))
               .then_inc(sem["xs3"], 16))
            bump("xs3", 16)
            op("gpsimd", lambda e: e.dma_start(out=wvbf_d[:, :], in_=w_d[D:2 * D, :])
               .then_inc(sem["wvs"], 16))
            bump("wvs", 16)
            op("gpsimd", lambda e: e.dma_start(out=wfbf_d[:, :], in_=wf_d)
               .then_inc(sem["wfs"], 16))
            bump("wfs", 16)

            ve_xt = [0] * NT
            pe_xt = [0] * NT
            xwait = {1: ("xs1", 16), 4: ("xs2", 16), 8: ("xs3", 16)}

            def emit_xtrans(t):
                if t == 0:
                    wait("tensor", "ds", ds_idn)
                    wait("tensor", "xs0", 16)
                if t in xwait:
                    wait("tensor", *xwait[t])
                if t >= 2:
                    wait("tensor", "ve", ve_xt[t - 2])
                for d in range(ND):
                    if t == 0 and d == 4:
                        wait("tensor", "xs0", 32)
                    def f(e, t=t, d=d, last=(d == ND - 1)):
                        ins = e.transpose(trp[t % 2][:, d, :], xtm[:, t, d * 128:(d + 1) * 128], idn[:])
                        if last:
                            ins.then_inc(sem["pe"], 1)
                    op("tensor", f)
                pe_xt[t] = bump("pe")
                wait("vector", "pe", pe_xt[t])
                op("vector", lambda e, t=t: e.tensor_copy(xfm[:, :, t * 128:(t + 1) * 128], trp[t % 2][:, :, :])
                   .then_inc(sem["ve"], 1))
                ve_xt[t] = bump("ve")

            for t in range(8):
                emit_xtrans(t)

            wait("sync", "wvs", 16)
            for dt in range(ND):
                op("sync", lambda e, dt=dt: e.dma_start_transpose(
                    wvt[:, dt, :], wvbf_d[:, dt * 128:(dt + 1) * 128]).then_inc(sem["wts"], 16))
                bump("wts", 16)
            wait("sync", "wfs", 16)
            for it in range(ND):
                op("sync", lambda e, it=it: e.dma_start_transpose(
                    wft[:, it, :], wfbf_d[:, it * 128:(it + 1) * 128]).then_inc(sem["fts"], 16))
                bump("fts", 16)

            wait("tensor", "ds", ds_bcast)
            wait("tensor", "ve", ve_init)
            op("tensor", lambda e: e.matmul(cps[0][:, 0:2], ones[:, :], scl_sb[:, :], start=True, stop=True))
            op("tensor", lambda e: e.matmul(cps[1][:, 0:512], ones[:, :], bias_sb[:, 0:512], start=True, stop=True))
            op("tensor", lambda e: e.matmul(cps[2][:, 0:512], ones[:, :], bias_sb[:, 512:1024], start=True, stop=True)
               .then_inc(sem["pe"], 1))
            pe_bcast = bump("pe")
            wait("vector", "pe", pe_bcast)
            op("vector", lambda e: e.tensor_copy(sclb[:, :], cps[0][:, 0:2]))
            op("vector", lambda e: e.drain())
            op("vector", lambda e: e.reciprocal(invh[:, :], sclb[:, 0:1]))
            op("vector", lambda e: e.drain())
            op("vector", lambda e: e.tensor_scalar_mul(invh[:, :], invh[:, :], 0.5))
            op("vector", lambda e: e.tensor_mul(ssb[:, :], sclb[:, 0:1], sclb[:, 1:2]))
            op("vector", lambda e: e.tensor_copy(biasb[:, 0:512], cps[1][:, 0:512]))
            op("vector", lambda e: e.tensor_copy(biasb[:, 512:1024], cps[2][:, 0:512])
               .then_inc(sem["ve"], 1))
            ve_bcast = bump("ve")

            ve_ct_half = {}
            kglob = 0
            for h in range(2):
                if h == 0:
                    wait("tensor", "ve", ve_xt[7])
                    wait("tensor", "ve", ve_bcast)
                else:
                    for t in range(8, NT):
                        emit_xtrans(t)
                    wait("tensor", "ve", ve_xt[NT - 1])
                for dt in range(ND):
                    pair = kglob % 2
                    if h == 0 and dt == 0:
                        wait("tensor", "wks", 16)
                    elif h == 0 and dt == 4:
                        wait("tensor", "wks", 32)
                    if kglob >= 2:
                        wait("tensor", "ve", ve_ct_half[kglob - 2])
                    for j in range(ND):
                        for cc in range(2):
                            c = h * 2 + cc
                            bank = cps[2 * pair + cc]
                            def f(e, dt=dt, j=j, c=c, bank=bank, last=(j == ND - 1 and cc == 1)):
                                ins = e.matmul(bank[:, 0:512],
                                               wbufA[:, j, dt * 128:(dt + 1) * 128],
                                               xfm[:, j, c * 512:(c + 1) * 512],
                                               start=(j == 0), stop=(j == ND - 1))
                                if last:
                                    ins.then_inc(sem["pe"], 1)
                            op("tensor", f)
                    pe_h = bump("pe")
                    wait("vector", "pe", pe_h)
                    for cc in range(2):
                        c = h * 2 + cc
                        bank = cps[2 * pair + cc]
                        def f(e, dt=dt, c=c, bank=bank, last=(cc == 1)):
                            ins = e.tensor_copy(ct[:, dt, c * 512:(c + 1) * 512], bank[:, 0:512])
                            if last:
                                ins.then_inc(sem["ve"], 1)
                        op("vector", f)
                    ve_ct_half[kglob] = bump("ve")
                    kglob += 1
            pe_ct_done = cnt["pe"]
            ve_ct_done = cnt["ve"]

            wait("tensor", "wts", 128)
            ve_v = [0] * NT
            for st in range(NT):
                pair = st % 2
                if st >= 2:
                    wait("tensor", "ve", ve_v[st - 2])
                elif st == 0:
                    wait("tensor", "ve", ve_ct_done)
                for d in range(ND):
                    for cc in range(2):
                        def f(e, st=st, d=d, cc=cc, pair=pair, last=(d == ND - 1 and cc == 1)):
                            ins = e.matmul(cps[2 * pair + cc][:, 0:512],
                                           xfm[:, d, st * 128:(st + 1) * 128],
                                           wvt[:, d, cc * 512:(cc + 1) * 512],
                                           start=(d == 0), stop=(d == ND - 1))
                            if last:
                                ins.then_inc(sem["pe"], 1)
                        op("tensor", f)
                pe_v = bump("pe")
                wait("vector", "pe", pe_v)
                for cc in range(2):
                    def f(e, st=st, cc=cc, pair=pair, last=(cc == 1)):
                        ins = e.tensor_copy(vtm[:, st, cc * 512:(cc + 1) * 512], cps[2 * pair + cc][:, 0:512])
                        if last:
                            ins.then_inc(sem["ve"], 1)
                    op("vector", f)
                ve_v[st] = bump("ve")

            wait("tensor", "fts", 128)
            ve_phase1_done = cnt["ve"]
            pe_phase1_done = cnt["pe"]

        ctx2 = ExitStack()
        with ctx2:
            sps = [ctx2.enter_context(nc.psum_tensor(f"sps{c}", [128, 512], F32)) for c in range(4)]
            pvps = [ctx2.enter_context(nc.psum_tensor(f"pvps{c}", [128, 512], F32)) for c in range(2)]
            trp2 = [ctx2.enter_context(nc.psum_tensor(f"trp2_{p}", [128, 8, 128], BF16)) for p in range(2)]

            s_sb = [ctx2.enter_context(nc.sbuf_tensor(f"s_sb{p}", [128, S_LEN], F32)) for p in range(2)]
            p_bf = [ctx2.enter_context(nc.sbuf_tensor(f"p_bf{p}", [128, S_LEN], BF16)) for p in range(2)]
            pt = ctx2.enter_context(nc.sbuf_tensor("pt", [128, NT, 128], BF16))
            xr = [ctx2.enter_context(nc.sbuf_tensor(f"xr{p}", [128, D], F32)) for p in range(2)]
            t_sb = [ctx2.enter_context(nc.sbuf_tensor(f"t_sb{p}", [128, D], F32)) for p in range(2)]
            y1 = [ctx2.enter_context(nc.sbuf_tensor(f"y1_{p}", [128, D], F32)) for p in range(2)]
            sq = [ctx2.enter_context(nc.sbuf_tensor(f"sq{p}", [128, D], F32)) for p in range(2)]
            q8 = [ctx2.enter_context(nc.sbuf_tensor("q8_0", [128, D], FP8))] * 2
            yqb2 = [ctx2.enter_context(nc.sbuf_tensor("yqb2_0", [128, D], BF16))] * 2
            yqb = [ctx2.enter_context(nc.sbuf_tensor(f"yqb{p}", [128, D], BF16)) for p in range(2)]
            yqt = ctx2.enter_context(nc.sbuf_tensor("yqt", [128, ND, 128], BF16))
            t2 = [ctx2.enter_context(nc.sbuf_tensor(f"t2_{p}", [128, D], F32)) for p in range(2)]
            y2 = [ctx2.enter_context(nc.sbuf_tensor(f"y2_{p}", [128, D], F32)) for p in range(2)]



            dg_xr = [0] * NT
            ve_pvevac = [0] * NT
            ac_scopy = [0] * NT
            ve_diag = [0] * NT
            ve_max = [0] * NT
            ac_exp = [0] * NT
            ve_tevac = [0] * NT
            pe_s = [0] * NT
            pe_pv = [0] * NT
            pe_yt = [0] * NT
            pe_ffn = [0] * NT
            ve_sevac_done = [0] * NT
            ve_ytevac = [0] * NT
            ve_ffnevac = [0] * NT
            ve_y2 = [0] * NT
            ve_q = [0] * NT
            ac_rstd1 = [0] * NT
            ac_rstd2 = [0] * NT
            do_store = [0] * NT
            ve_nm1 = [0] * NT
            ve_pvstt_c = [[0, 0] for _ in range(NT)]
            pe_ffn_c = [[0, 0] for _ in range(NT)]
            ve_scopyd = [0] * NT
            tev_groups = [None] * NT
            pe_s_chunk = [None] * NT
            ac_scopy_c = [[0] * 4 for _ in range(NT)]
            pe_yt_a = [0] * NT
            ve_nm2 = [0] * NT
            ac_exp_a = [0] * NT
            exp_split = [False] * NT
            ve_ytev_a = [0] * NT
            ve_y1 = [0] * NT
            yt_tb = [0] * NT
            trp2_free2 = [("ac", 0), ("ac", 0)]
            tb_ctr = 0

            def emit_xr(i):
                nonlocal dg_xr
                k = f"xr{i % 4}"
                if i >= 2:
                    wait("gpsimd", "vp", ve_pvevac[i - 2])
                op("gpsimd", lambda e, i=i, k=k: e.dma_start(out=xr[i % 2][:, :], in_=x_d[i * 128:(i + 1) * 128, :])
                   .then_inc(sem[k], 16))
                v1 = bump(k, 16)
                wait("gpsimd", k, v1)
                op("gpsimd", lambda e, i=i, k=k: e.dma_start(out=xr[i % 2][:, :], in_=x_d[i * 128:(i + 1) * 128, :],
                                                             accum_op=ALU.add).then_inc(sem[k], 16))
                dg_xr[i] = bump(k, 16)

            def emit_S(i):
                par = i % 2
                nch = (i + 1) * 128 // 512 + (1 if ((i + 1) * 128) % 512 else 0)
                if i == 0:
                    wait("tensor", "ve", ve_phase1_done)
                else:
                    wait("tensor", "ac", ac_scopy[i - 1])
                pe_s_chunk[i] = [0] * nch
                for c in range(nch):
                    w = min(512, (i + 1) * 128 - c * 512)
                    for j in range(ND):
                        def f(e, i=i, j=j, c=c, w=w, last=(j == ND - 1)):
                            ins = e.matmul(sps[c][:, 0:w],
                                           ct[:, j, i * 128:(i + 1) * 128],
                                           xfm[:, j, c * 512:c * 512 + w],
                                           start=(j == 0), stop=(j == ND - 1))
                            if last:
                                ins.then_inc(sem["pe"], 1)
                        op("tensor", f)
                    pe_s_chunk[i][c] = bump("pe")
                pe_s[i] = pe_s_chunk[i][-1]

            def emit_scopy(i):
                par = i % 2
                W = (i + 1) * 128
                nch = W // 512 + (1 if W % 512 else 0)
                if i >= 2:
                    wait("scalar", "vs", ve_max[i - 2])
                for c in range(nch):
                    w = min(512, W - c * 512)
                    wait("scalar", "pe", pe_s_chunk[i][c])
                    def f(e, c=c, w=w, par=par):
                        e.copy(s_sb[par][:, c * 512:c * 512 + w], sps[c][:, 0:w]).then_inc(sem["ac"], 1)
                    op("scalar", f)
                    ac_scopy_c[i][c] = bump("ac")
                ac_scopy[i] = ac_scopy_c[i][nch - 1]
                ve_scopyd[i] = 0

            def emit_diag(i):
                par = i % 2
                W = (i + 1) * 128
                nch = W // 512 + (1 if W % 512 else 0)
                dcol = i * 128
                if i == 0:
                    wait("vector", "ds", ds_all)
                if i >= 2:
                    wait("vector", "ac", ac_exp[i - 2])
                for c in range(nch - 1):
                    wait("vector", "ac", ac_scopy_c[i][c])
                    def f(e, c=c, par=par):
                        e.reduce_max(pmax[:, c:c + 1], s_sb[par][:, c * 512:(c + 1) * 512],
                                     axis=AXX, negate=True)
                    op("vector", f)
                wait("vector", "ac", ac_scopy_c[i][nch - 1])
                op("vector", lambda e: e.drain())
                op("vector", lambda e, par=par, dcol=dcol: e.tensor_add(
                    s_sb[par][:, dcol:dcol + 128], s_sb[par][:, dcol:dcol + 128], maskd[:])
                    .then_inc(sem["vs"], 1))
                ve_diag[i] = bump("vs")

            def emit_max(i):
                par = i % 2
                W = (i + 1) * 128
                nch = W // 512 + (1 if W % 512 else 0)
                w = W - (nch - 1) * 512
                op("vector", lambda e: e.drain())
                def f(e, c=nch - 1, w=w, par=par):
                    e.reduce_max(pmax[:, c:c + 1], s_sb[par][:, c * 512:c * 512 + w],
                                 axis=AXX, negate=True)
                op("vector", f)
                op("vector", lambda e: e.drain())
                op("vector", lambda e, par=par, nch=nch: e.tensor_reduce(
                    stat(NMAX, par), pmax[:, 0:nch], AXX, ALU.min).then_inc(sem["vs"], 1))
                ve_max[i] = bump("vs")
            def emit_exp(i):
                par = i % 2
                W = (i + 1) * 128
                Wa = min(W, 1024)
                wait("scalar", "vs", ve_max[i])
                if i >= 2:
                    wait("scalar", "pe", pe_t_last[i - 2])
                op("scalar", lambda e, par=par, Wa=Wa: e.activation(
                    p_bf[par][:, 0:Wa], s_sb[par][:, 0:Wa], AF.Exp,
                    bias=stat(NMAX, par), accum_out=stat(LSUM, par)).then_inc(sem["ac"], 1))
                ac_exp_a[i] = bump("ac")
                if W > 1024:
                    op("scalar", lambda e: e.drain())
                    op("scalar", lambda e, par=par, W=W: e.activation(
                        p_bf[par][:, 1024:W], s_sb[par][:, 1024:W], AF.Exp,
                        bias=stat(NMAX, par), accum_out=stat(LSUMB, par)).then_inc(sem["ac"], 1))
                    ac_exp[i] = bump("ac")
                else:
                    ac_exp[i] = ac_exp_a[i]
                exp_split[i] = (W > 1024)

            def emit_recip(i):
                par = i % 2
                wait("vector", "ac", ac_exp[i])
                if exp_split[i]:
                    op("vector", lambda e, par=par: e.tensor_add(stat(LSUM, par), stat(LSUM, par),
                                                                 stat(LSUMB, par)))
                    op("vector", lambda e: e.drain())
                op("vector", lambda e, par=par: e.reciprocal(stat(RECI, par), stat(LSUM, par))
                   .then_inc(sem["vs"], 1))
                op("vector", lambda e: e.drain())
                bump("vs")

            pe_t_last = [0] * NT

            def emit_T(i):
                nonlocal tb_ctr
                ntile = i + 1
                par = i % 2
                col = 0
                evs = []
                while col < ntile:
                    ng = min(4, ntile - col)
                    tb = tb_ctr % 2
                    tb_ctr += 1
                    wait("tensor", trp2_free2[tb][0], trp2_free2[tb][1])
                    if col < 8:
                        wait("tensor", "ac", ac_exp_a[i])
                    else:
                        wait("tensor", "ac", ac_exp[i])
                    for u in range(ng):
                        def f(e, par=par, col=col, u=u, tb=tb, last=(u == ng - 1)):
                            ins = e.transpose(trp2[tb][:, u, :],
                                              p_bf[par][:, (col + u) * 128:(col + u + 1) * 128], idn[:])
                            if last:
                                ins.then_inc(sem["pe"], 1)
                        op("tensor", f)
                    pe_g = bump("pe")
                    wait("scalar", "pe", pe_g)
                    op("scalar", lambda e, col=col, ng=ng, tb=tb: e.copy(
                        pt[:, col:col + ng, :], trp2[tb][:, 0:ng, :]).then_inc(sem["ac"], 1))
                    trp2_free2[tb] = ("ac", bump("ac"))
                    evs.append(trp2_free2[tb][1])
                    col += ng
                pe_t_last[i] = pe_g
                tev_groups[i] = list(evs)
                ve_tevac[i] = evs[-1]

            pv_free_ve = [("vp", 0)]

            def emit_PV(i):
                par = i % 2
                wait("tensor", pv_free_ve[0][0], pv_free_ve[0][1])
                for t in range(i + 1):
                    if t % 4 == 0:
                        wait("tensor", "ac", tev_groups[i][t // 4])
                    for cc in range(2):
                        def f(e, t=t, cc=cc, i=i, last=(t == i and cc == 1)):
                            ins = e.matmul(pvps[cc][:, 0:512], pt[:, t, :],
                                           vtm[:, t, cc * 512:(cc + 1) * 512],
                                           start=(t == 0), stop=(t == i))
                            if last:
                                ins.then_inc(sem["pe"], 1)
                        op("tensor", f)
                pe_pv[i] = bump("pe")

            def emit_pvstt(i):
                par = i % 2
                wait("vector", "pe", pe_pv[i])
                wait("vector", f"xr{i % 4}", dg_xr[i])
                for cc in range(2):
                    def f(e, cc=cc, par=par):
                        e.scalar_tensor_tensor(t_sb[par][:, cc * 512:(cc + 1) * 512],
                                               pvps[cc][:, 0:512], stat(RECI, par),
                                               xr[par][:, cc * 512:(cc + 1) * 512],
                                               ALU.mult, ALU.add,
                                               accum_out=stat(ST1 if cc == 0 else ST1B, par)
                                               ).then_inc(sem["vp"], 1)
                    op("vector", f)
                    ve_pvstt_c[i][cc] = bump("vp")
                ve_pvevac[i] = ve_pvstt_c[i][1]
                pv_free_ve[0] = ("vp", ve_pvevac[i])
                op("vector", lambda e: e.drain())
                op("vector", lambda e, par=par: e.tensor_scalar(stat(NM1, par), stat(ST1, par),
                                                                stat(ST1B, par), -1.0 / 1024.0,
                                                                ALU.add, ALU.mult).then_inc(sem["vp"], 1))
                ve_nm1[i] = bump("vp")

            def emit_ln1(i):
                par = i % 2
                wait("scalar", "vp", ve_nm1[i])
                op("scalar", lambda e, par=par: e.activation(sq[par][:, :], t_sb[par][:, :], AF.Square,
                                                             bias=stat(NM1, par), accum_out=stat(VS1, par)))
                op("scalar", lambda e: e.drain())
                op("scalar", lambda e, par=par: e.activation(stat(U1, par), stat(VS1, par), AF.Ln,
                                                             scale=cst[:, 1:2], bias=cst[:, 0:1]))
                op("scalar", lambda e: e.drain())
                op("scalar", lambda e, par=par: e.activation(stat(RS1, par), stat(U1, par), AF.Exp,
                                                             scale=cst[:, 2:3]).then_inc(sem["ac"], 1))
                ac_rstd1[i] = bump("ac")

            def emit_y1q(i):
                par = i % 2
                wait("vector", "ac", ac_rstd1[i])
                op("vector", lambda e, par=par: e.tensor_scalar(y1[par][:, :], t_sb[par][:, :],
                                                                stat(NM1, par), stat(RS1, par),
                                                                ALU.add, ALU.mult).then_inc(sem["vp"], 1))
                ve_y1[i] = bump("vp")
                op("vector", lambda e: e.drain())
                op("vector", lambda e, par=par: e.tensor_scalar(sq[par][:, :], y1[par][:, :],
                                                                invh[:, 0:1], 224.0, ALU.mult, ALU.min))
                op("vector", lambda e: e.drain())
                op("vector", lambda e, par=par: e.tensor_scalar(q8[par][:, :], sq[par][:, :],
                                                                -224.0, None, ALU.max))
                op("vector", lambda e: e.drain())
                if i >= 2:
                    wait("vector", "qs0", 16 * (i - 1))
                op("vector", lambda e, par=par: e.tensor_scalar(yqb[par][:, :], q8[par][:, :],
                                                                2.0, None, ALU.mult).then_inc(sem["vp"], 1))
                ve_q[i] = bump("vp")
                wait("sync", "vp", ve_q[i])
                if i >= 1:
                    wait("sync", "pe", pe_yt[i - 1])
                op("sync", lambda e, par=par: e.dma_start(
                    out=yqb2[par][:, :].bitcast(mybir.dt.int16),
                    in_=yqb[par][:, :].bitcast(mybir.dt.int16)).then_inc(sem["qs0"], 16))
                bump("qs0", 16)

            def emit_yT(i):
                nonlocal tb_ctr
                par = i % 2
                tb1 = tb_ctr % 2
                tb_ctr += 1
                tb2 = tb_ctr % 2
                tb_ctr += 1
                yt_tb[i] = (tb1, tb2)
                wait("tensor", "qs0", 16 * (i + 1))
                wait("tensor", trp2_free2[tb1][0], trp2_free2[tb1][1])
                for u in range(4):
                    def f(e, par=par, u=u, tb=tb1, last=(u == 3)):
                        ins = e.transpose(trp2[tb][:, u, :], yqb2[par][:, u * 128:(u + 1) * 128], idn[:])
                        if last:
                            ins.then_inc(sem["pe"], 1)
                    op("tensor", f)
                pe_yt_a[i] = bump("pe")
                wait("tensor", trp2_free2[tb2][0], trp2_free2[tb2][1])
                for u in range(4, 8):
                    def f(e, par=par, u=u, tb=tb2, last=(u == 7)):
                        ins = e.transpose(trp2[tb][:, u - 4, :], yqb2[par][:, u * 128:(u + 1) * 128], idn[:])
                        if last:
                            ins.then_inc(sem["pe"], 1)
                    op("tensor", f)
                pe_yt[i] = bump("pe")

            def emit_ytevac(i):
                tb1, tb2 = yt_tb[i]
                wait("scalar", "pe", pe_yt_a[i])
                op("scalar", lambda e, tb=tb1: e.copy(yqt[:, 0:4, :], trp2[tb][:, 0:4, :])
                   .then_inc(sem["ac"], 1))
                trp2_free2[tb1] = ("ac", bump("ac"))
                ve_ytev_a[i] = trp2_free2[tb1][1]
                wait("scalar", "pe", pe_yt[i])
                op("scalar", lambda e, tb=tb2: e.copy(yqt[:, 4:8, :], trp2[tb][:, 0:4, :])
                   .then_inc(sem["ac"], 1))
                trp2_free2[tb2] = ("ac", bump("ac"))
                ve_ytevac[i] = trp2_free2[tb2][1]

            def emit_FFN(i):
                wait("tensor", "ac", ve_ytev_a[i])
                pe_ffn_c[i] = [0, 0]
                for cc in range(2):
                    wait("tensor", "vp", ve_pvstt_c[min(i + 1, NT - 1)][cc])
                    if cc == 0:
                        pass
                    else:
                        wait("tensor", "ac", ve_ytevac[i])
                    for ic in range(ND):
                        if cc == 0 and ic == 4:
                            wait("tensor", "ac", ve_ytevac[i])
                        def f(e, ic=ic, cc=cc, last=(ic == ND - 1)):
                            ins = e.matmul(pvps[cc][:, 0:512], yqt[:, ic, :],
                                           wft[:, ic, cc * 512:(cc + 1) * 512],
                                           start=(ic == 0), stop=(ic == ND - 1))
                            if last:
                                ins.then_inc(sem["pe"], 1)
                        op("tensor", f)
                    pe_ffn_c[i][cc] = bump("pe")
                pe_ffn[i] = pe_ffn_c[i][1]

            def emit_ffnstt(i):
                par = i % 2
                op("vector", lambda e, par=par: e.tensor_add(sq[par][:, :], y1[par][:, :], biasb[:, :]))
                op("vector", lambda e: e.drain())
                if i >= 2:
                    wait("vector", f"do{i % 2}", do_store[i - 2])
                for cc in range(2):
                    wait("vector", "pe", pe_ffn_c[i][cc])
                    def f(e, cc=cc, par=par, last=(cc == 1)):
                        ins = e.scalar_tensor_tensor(t2[par][:, cc * 512:(cc + 1) * 512],
                                                     pvps[cc][:, 0:512], ssb[:, 0:1],
                                                     sq[par][:, cc * 512:(cc + 1) * 512],
                                                     ALU.mult, ALU.add,
                                                     accum_out=stat(ST2 if cc == 0 else ST2B, par))
                        if last:
                            ins.then_inc(sem["vf"], 1)
                    op("vector", f)
                ve_ffnevac[i] = bump("vf")
                pv_free_ve[0] = ("vf", ve_ffnevac[i])
                op("vector", lambda e: e.drain())
                op("vector", lambda e, par=par: e.tensor_scalar(stat(NM2, par), stat(ST2, par),
                                                                stat(ST2B, par), -1.0 / 1024.0,
                                                                ALU.add, ALU.mult).then_inc(sem["vf"], 1))
                ve_nm2[i] = bump("vf")

            def emit_ln2y2(i):
                par = i % 2
                wait("scalar", "vf", ve_nm2[i])
                op("scalar", lambda e, par=par: e.activation(sq[par][:, :], t2[par][:, :], AF.Square,
                                                             bias=stat(NM2, par), accum_out=stat(VS2, par)))
                op("scalar", lambda e: e.drain())
                op("scalar", lambda e, par=par: e.activation(stat(U1, par), stat(VS2, par), AF.Ln,
                                                             scale=cst[:, 1:2], bias=cst[:, 0:1]))
                op("scalar", lambda e: e.drain())
                op("scalar", lambda e, par=par: e.activation(stat(RS2, par), stat(U1, par), AF.Exp,
                                                             scale=cst[:, 2:3]).then_inc(sem["ac"], 1))
                ac_rstd2[i] = bump("ac")
                wait("vector", "ac", ac_rstd2[i])
                op("vector", lambda e, par=par: e.tensor_scalar(y2[par][:, :], t2[par][:, :],
                                                                stat(NM2, par), stat(RS2, par),
                                                                ALU.add, ALU.mult).then_inc(sem["vf"], 1))
                ve_y2[i] = bump("vf")
                wait("sync", "vf", ve_y2[i])
                op("sync", lambda e, i=i, par=par: e.dma_start(out=y_d[i * 128:(i + 1) * 128, :], in_=y2[par][:, :])
                   .then_inc(sem[f"do{par}"], 16))
                do_store[i] = bump(f"do{par}", 16)

            if do_phase2 and n_win > 0:
                wait("gpsimd", "pe", pe_phase1_done)
                wait("gpsimd", "ve", ve_phase1_done)
                emit_xr(0)
                if n_win > 1:
                    emit_xr(1)
                emit_S(0)
                emit_scopy(0)
                emit_diag(0)
                emit_max(0)
                emit_exp(0)
                emit_recip(0)
                emit_T(0)
                for i in range(n_win):
                    if i + 1 < n_win:
                        emit_S(i + 1)
                        emit_scopy(i + 1)
                        emit_diag(i + 1)
                        emit_max(i + 1)
                        emit_exp(i + 1)
                    emit_PV(i)
                    emit_pvstt(i)
                    if i >= 1:
                        emit_yT(i - 1)
                        emit_ytevac(i - 1)
                    emit_ln1(i)
                    if i + 2 < n_win:
                        emit_xr(i + 2)
                    if i >= 1:
                        emit_FFN(i - 1)
                        emit_ffnstt(i - 1)
                    emit_y1q(i)
                    if i + 1 < n_win:
                        emit_recip(i + 1)
                        emit_T(i + 1)
                    if i >= 1:
                        emit_ln2y2(i - 1)
                emit_yT(n_win - 1)
                emit_ytevac(n_win - 1)
                emit_FFN(n_win - 1)
                emit_ffnstt(n_win - 1)
                emit_ln2y2(n_win - 1)
                wait("sync", "do0", cnt["do0"])
                wait("sync", "do1", cnt["do1"])
            else:
                wait("vector", "ve", ve_phase1_done)
                op("vector", lambda e: e.tensor_copy(t2[0][:, :], ct[:, 0, 0:1024]).then_inc(sem["ve"], 1))
                vend = bump("ve")
                wait("sync", "ve", vend)
                op("sync", lambda e: e.dma_start(out=y_d[0:128, :], in_=t2[0][:, :]).then_inc(sem["do0"], 16))
                bump("do0", 16)
                wait("sync", "do0", cnt["do0"])

        block = ctx.enter_context(nc.Block())

        def runner(name):
            def run(eng):
                for e, fn in sched:
                    if e == name:
                        fn(eng)
            return run
        block.sync(runner("sync"))
        block.gpsimd(runner("gpsimd"))
        block.tensor(runner("tensor"))
        block.vector(runner("vector"))
        block.scalar(runner("scalar"))

    return nc


_NC_CACHE = [None]


def kernel(L0_attn_x, L0_attn_W, L0_ffn_weight_fp8, L0_ffn_bias, L0_ffn_input_scale, L0_ffn_weight_scale):
    x = np.ascontiguousarray(np.asarray(L0_attn_x, dtype=np.float32))
    W = np.ascontiguousarray(np.asarray(L0_attn_W, dtype=np.float32))
    Wf = np.ascontiguousarray(np.asarray(L0_ffn_weight_fp8, dtype=np.float32))
    bias = np.ascontiguousarray(np.asarray(L0_ffn_bias, dtype=np.float32).reshape(1, D))
    scl = np.ascontiguousarray(np.stack([np.asarray(L0_ffn_input_scale, dtype=np.float32).reshape(()),
                                         np.asarray(L0_ffn_weight_scale, dtype=np.float32).reshape(())]).reshape(1, 2))
    r = np.arange(128)
    mask = np.where(r[None, :] <= r[:, None], np.float32(0.0), np.float32(MASK_NEG)).astype(np.float32)
    idn = np.eye(128, dtype=ml_dtypes.bfloat16)

    if _NC_CACHE[0] is None:
        _NC_CACHE[0] = build_nc()
    nc = _NC_CACHE[0]
    in_maps = [{"x": x[b], "w": W, "wf": Wf, "bias": bias, "scl": scl, "mask": mask, "idn": idn}
               for b in range(8)]
    res = run_bass_kernel_spmd(nc, in_maps, core_ids=list(range(8)))
    return np.stack([res.results[b]["y"] for b in range(8)], axis=0)




# version 44
# speedup vs baseline: 1.1079x; 1.0054x over previous
import math
import numpy as np
import ml_dtypes
from contextlib import ExitStack
import concourse.bass as bass
import concourse.mybir as mybir
from concourse.bass_utils import run_bass_kernel_spmd

F32 = mybir.dt.float32
BF16 = mybir.dt.bfloat16
FP8 = mybir.dt.float8e4
AF = mybir.ActivationFunctionType
ALU = mybir.AluOpType
AXX = mybir.AxisListType.X

S_LEN = 2048
D = 1024
NT = 16
ND = 8
MASK_NEG = -1.0e4


def build_nc(n_win=NT, do_phase2=True, last_parts=3, skip=()):
    nc = bass.Bass("TRN2", target_bir_lowering=False)

    x_d = nc.dram_tensor("x", [S_LEN, D], F32, kind="ExternalInput").ap()
    w_d = nc.dram_tensor("w", [2 * D, D], F32, kind="ExternalInput").ap()
    wf_d = nc.dram_tensor("wf", [D, D], F32, kind="ExternalInput").ap()
    bias_d = nc.dram_tensor("bias", [1, D], F32, kind="ExternalInput").ap()
    scl_d = nc.dram_tensor("scl", [1, 2], F32, kind="ExternalInput").ap()
    mask_d = nc.dram_tensor("mask", [128, 128], F32, kind="ExternalInput").ap()
    idn_d = nc.dram_tensor("idn", [128, 128], BF16, kind="ExternalInput").ap()
    y_d = nc.dram_tensor("y", [S_LEN, D], F32, kind="ExternalOutput").ap()
    wvbf_d = nc.dram_tensor("wvbf_scratch", [D, D], BF16).ap()
    wfbf_d = nc.dram_tensor("wfbf_scratch", [D, D], BF16).ap()

    sched = []

    def op(eng, fn):
        sched.append((eng, fn))

    cnt = {"pe": 0, "ve": 0, "ac": 0, "ds": 0, "vs": 0, "vt": 0, "vp": 0, "vf": 0,
           "xs0": 0, "xs1": 0, "xs2": 0, "xs3": 0, "wks": 0, "wvs": 0, "wfs": 0,
           "xr0": 0, "xr1": 0, "xr2": 0, "xr3": 0, "do0": 0, "do1": 0, "po": 0,
           "xfs": 0, "wts": 0, "fts": 0, "qs0": 0}

    ctx = ExitStack()
    with ctx:
        sem = {k: ctx.enter_context(nc.semaphore(k)) for k in cnt}

        def wait(eng, key, val):
            if val <= 0:
                return
            op(eng, lambda e, s=sem[key], v=val: e.wait_ge(s, v))

        def bump(key, k=1):
            cnt[key] += k
            return cnt[key]

        xfm = ctx.enter_context(nc.sbuf_tensor("xfm", [128, ND, S_LEN], BF16))
        ct = ctx.enter_context(nc.sbuf_tensor("ct", [128, ND, S_LEN], BF16))
        vtm = ctx.enter_context(nc.sbuf_tensor("vtm", [128, NT, D], BF16))
        wft = ctx.enter_context(nc.sbuf_tensor("wft", [128, ND, D], BF16))
        biasb = ctx.enter_context(nc.sbuf_tensor("biasb", [128, D], F32))
        maskd = ctx.enter_context(nc.sbuf_tensor("maskd", [128, 128], F32))
        idn = ctx.enter_context(nc.sbuf_tensor("idn_sb", [128, 128], BF16))
        ones = ctx.enter_context(nc.sbuf_tensor("ones", [1, 128], F32))
        bias_sb = ctx.enter_context(nc.sbuf_tensor("bias_sb", [1, D], F32))
        scl_sb = ctx.enter_context(nc.sbuf_tensor("scl_sb", [1, 2], F32))
        sclb = ctx.enter_context(nc.sbuf_tensor("sclb", [128, 2], F32))
        invh = ctx.enter_context(nc.sbuf_tensor("invh", [128, 1], F32))
        ssb = ctx.enter_context(nc.sbuf_tensor("ssb", [128, 1], F32))
        cst = ctx.enter_context(nc.sbuf_tensor("cst", [128, 3], F32))
        stats = ctx.enter_context(nc.sbuf_tensor("stats", [128, 30], F32))
        pmax = ctx.enter_context(nc.sbuf_tensor("pmax", [128, 8], F32))

        def stat(k, par):
            return stats[:, 2 * k + par: 2 * k + par + 1]
        NMAX, LSUM, RECI, ST1, NM1, VS1, U1, RS1, ST2, NM2, VS2, RS2, ST1B, ST2B = range(14)
        LSUMB = 13
        LSUMB = 14

        op("sync", lambda e: e.dma_start(out=idn[:], in_=idn_d).then_inc(sem["ds"], 16))
        op("sync", lambda e: e.dma_start(out=bias_sb[:], in_=bias_d).then_inc(sem["ds"], 16))
        op("sync", lambda e: e.dma_start(out=scl_sb[:], in_=scl_d).then_inc(sem["ds"], 16))
        op("sync", lambda e: e.dma_start(out=maskd[:], in_=mask_d).then_inc(sem["ds"], 16))
        ds_idn = 16
        ds_bcast = 48
        ds_all = 64

        op("vector", lambda e: e.memset(cst[:, 0:1], 1e-5))
        op("vector", lambda e: e.memset(cst[:, 1:2], 1.0 / 1024.0))
        op("vector", lambda e: e.memset(cst[:, 2:3], -0.5))
        op("vector", lambda e: e.memset(ones[:, :], 1.0).then_inc(sem["ve"], 1))
        ve_init = bump("ve")

        def _preload(e):
            try:
                from concourse.hw_specs import get_activation_tables
                need = {AF.Exp, AF.Ln, AF.Copy, AF.Square}
                for idx, funcs in enumerate(get_activation_tables(e.bass.m.arch).values()):
                    if need.issubset(funcs):
                        e.add_instruction(mybir.InstLoadActFuncSet(
                            name=e.bass.get_next_instruction_name(),
                            act_func_set_id=idx, ins=[], outs=[]))
                        return
            except Exception:
                pass
        op("scalar", _preload)

        ctx1 = ExitStack()
        with ctx1:
            wbufA = ctx1.enter_context(nc.sbuf_tensor("wbufA", [128, ND, D], BF16))
            wvt = ctx1.enter_context(nc.sbuf_tensor("wvt", [128, ND, D], BF16))
            xtm = ctx1.enter_context(nc.sbuf_tensor("xtm", [128, NT, D], BF16))
            trp = [ctx1.enter_context(nc.psum_tensor(f"trp{p}", [128, ND, 128], BF16)) for p in range(2)]
            cps = [ctx1.enter_context(nc.psum_tensor(f"cps{c}", [128, 512], F32)) for c in range(4)]

            op("gpsimd", lambda e: e.dma_start(
                out=xtm[:, 0:1, 0:512],
                in_=x_d[0:128, 0:512].rearrange("(t p) d -> p t d", p=128)).then_inc(sem["xs0"], 16))
            bump("xs0", 16)
            op("gpsimd", lambda e: e.dma_start(
                out=xtm[:, 0:1, 512:1024],
                in_=x_d[0:128, 512:1024].rearrange("(t p) d -> p t d", p=128)).then_inc(sem["xs0"], 16))
            bump("xs0", 16)
            for (t0, t1, k) in [(1, 4, "xs1"), (4, 8, "xs2")]:
                def f(e, t0=t0, t1=t1, k=k):
                    e.dma_start(out=xtm[:, t0:t1, :],
                                in_=x_d[t0 * 128:t1 * 128, :].rearrange("(t p) d -> p t d", p=128)
                                ).then_inc(sem[k], 16)
                op("gpsimd", f)
                bump(k, 16)
            for h in range(2):
                op("gpsimd", lambda e, h=h: e.dma_start(
                    out=wbufA[:, :, h * 512:(h + 1) * 512],
                    in_=w_d[0:D, h * 512:(h + 1) * 512].rearrange("(t p) d -> p t d", p=128))
                   .then_inc(sem["wks"], 16))
                bump("wks", 16)
            op("gpsimd", lambda e: e.dma_start(out=xtm[:, 8:16, :],
                                               in_=x_d[8 * 128:16 * 128, :].rearrange("(t p) d -> p t d", p=128))
               .then_inc(sem["xs3"], 16))
            bump("xs3", 16)
            op("gpsimd", lambda e: e.dma_start(out=wvbf_d[:, :], in_=w_d[D:2 * D, :])
               .then_inc(sem["wvs"], 16))
            bump("wvs", 16)
            op("gpsimd", lambda e: e.dma_start(out=wfbf_d[:, :], in_=wf_d)
               .then_inc(sem["wfs"], 16))
            bump("wfs", 16)

            ve_xt = [0] * NT
            pe_xt = [0] * NT
            xwait = {1: ("xs1", 16), 4: ("xs2", 16), 8: ("xs3", 16)}

            def emit_xtrans(t):
                if t == 0:
                    wait("tensor", "ds", ds_idn)
                    wait("tensor", "xs0", 16)
                if t in xwait:
                    wait("tensor", *xwait[t])
                if t >= 2:
                    wait("tensor", "ve", ve_xt[t - 2])
                for d in range(ND):
                    if t == 0 and d == 4:
                        wait("tensor", "xs0", 32)
                    def f(e, t=t, d=d, last=(d == ND - 1)):
                        ins = e.transpose(trp[t % 2][:, d, :], xtm[:, t, d * 128:(d + 1) * 128], idn[:])
                        if last:
                            ins.then_inc(sem["pe"], 1)
                    op("tensor", f)
                pe_xt[t] = bump("pe")
                wait("vector", "pe", pe_xt[t])
                op("vector", lambda e, t=t: e.tensor_copy(xfm[:, :, t * 128:(t + 1) * 128], trp[t % 2][:, :, :])
                   .then_inc(sem["ve"], 1))
                ve_xt[t] = bump("ve")

            for t in range(8):
                emit_xtrans(t)

            wait("sync", "wvs", 16)
            for dt in range(ND):
                op("sync", lambda e, dt=dt: e.dma_start_transpose(
                    wvt[:, dt, :], wvbf_d[:, dt * 128:(dt + 1) * 128]).then_inc(sem["wts"], 16))
                bump("wts", 16)
            wait("sync", "wfs", 16)
            for it in range(ND):
                op("sync", lambda e, it=it: e.dma_start_transpose(
                    wft[:, it, :], wfbf_d[:, it * 128:(it + 1) * 128]).then_inc(sem["fts"], 16))
                bump("fts", 16)

            wait("tensor", "ds", ds_bcast)
            wait("tensor", "ve", ve_init)
            op("tensor", lambda e: e.matmul(cps[0][:, 0:2], ones[:, :], scl_sb[:, :], start=True, stop=True))
            op("tensor", lambda e: e.matmul(cps[1][:, 0:512], ones[:, :], bias_sb[:, 0:512], start=True, stop=True))
            op("tensor", lambda e: e.matmul(cps[2][:, 0:512], ones[:, :], bias_sb[:, 512:1024], start=True, stop=True)
               .then_inc(sem["pe"], 1))
            pe_bcast = bump("pe")
            wait("vector", "pe", pe_bcast)
            op("vector", lambda e: e.tensor_copy(sclb[:, :], cps[0][:, 0:2]))
            op("vector", lambda e: e.drain())
            op("vector", lambda e: e.reciprocal(invh[:, :], sclb[:, 0:1]))
            op("vector", lambda e: e.drain())
            op("vector", lambda e: e.tensor_scalar_mul(invh[:, :], invh[:, :], 0.5))
            op("vector", lambda e: e.tensor_mul(ssb[:, :], sclb[:, 0:1], sclb[:, 1:2]))
            op("vector", lambda e: e.tensor_copy(biasb[:, 0:512], cps[1][:, 0:512]))
            op("vector", lambda e: e.tensor_copy(biasb[:, 512:1024], cps[2][:, 0:512])
               .then_inc(sem["ve"], 1))
            ve_bcast = bump("ve")

            ve_ct_half = {}
            kglob = 0
            for h in range(2):
                if h == 0:
                    wait("tensor", "ve", ve_xt[7])
                    wait("tensor", "ve", ve_bcast)
                else:
                    for t in range(8, NT):
                        emit_xtrans(t)
                    wait("tensor", "ve", ve_xt[NT - 1])
                for dt in range(ND):
                    pair = kglob % 2
                    if h == 0 and dt == 0:
                        wait("tensor", "wks", 16)
                    elif h == 0 and dt == 4:
                        wait("tensor", "wks", 32)
                    if kglob >= 2:
                        wait("tensor", "ve", ve_ct_half[kglob - 2])
                    for j in range(ND):
                        for cc in range(2):
                            c = h * 2 + cc
                            bank = cps[2 * pair + cc]
                            def f(e, dt=dt, j=j, c=c, bank=bank, last=(j == ND - 1 and cc == 1)):
                                ins = e.matmul(bank[:, 0:512],
                                               wbufA[:, j, dt * 128:(dt + 1) * 128],
                                               xfm[:, j, c * 512:(c + 1) * 512],
                                               start=(j == 0), stop=(j == ND - 1))
                                if last:
                                    ins.then_inc(sem["pe"], 1)
                            op("tensor", f)
                    pe_h = bump("pe")
                    wait("vector", "pe", pe_h)
                    for cc in range(2):
                        c = h * 2 + cc
                        bank = cps[2 * pair + cc]
                        def f(e, dt=dt, c=c, bank=bank, last=(cc == 1)):
                            ins = e.tensor_copy(ct[:, dt, c * 512:(c + 1) * 512], bank[:, 0:512])
                            if last:
                                ins.then_inc(sem["ve"], 1)
                        op("vector", f)
                    ve_ct_half[kglob] = bump("ve")
                    kglob += 1
            pe_ct_done = cnt["pe"]
            ve_ct_done = cnt["ve"]

            wait("tensor", "wts", 128)
            ve_v = [0] * NT
            for st in range(NT):
                pair = st % 2
                if st >= 2:
                    wait("tensor", "ve", ve_v[st - 2])
                elif st == 0:
                    wait("tensor", "ve", ve_ct_done)
                for d in range(ND):
                    for cc in range(2):
                        def f(e, st=st, d=d, cc=cc, pair=pair, last=(d == ND - 1 and cc == 1)):
                            ins = e.matmul(cps[2 * pair + cc][:, 0:512],
                                           xfm[:, d, st * 128:(st + 1) * 128],
                                           wvt[:, d, cc * 512:(cc + 1) * 512],
                                           start=(d == 0), stop=(d == ND - 1))
                            if last:
                                ins.then_inc(sem["pe"], 1)
                        op("tensor", f)
                pe_v = bump("pe")
                wait("vector", "pe", pe_v)
                for cc in range(2):
                    def f(e, st=st, cc=cc, pair=pair, last=(cc == 1)):
                        ins = e.tensor_copy(vtm[:, st, cc * 512:(cc + 1) * 512], cps[2 * pair + cc][:, 0:512])
                        if last:
                            ins.then_inc(sem["ve"], 1)
                    op("vector", f)
                ve_v[st] = bump("ve")

            wait("tensor", "fts", 128)
            ve_phase1_done = cnt["ve"]
            pe_phase1_done = cnt["pe"]

        ctx2 = ExitStack()
        with ctx2:
            sps = [ctx2.enter_context(nc.psum_tensor(f"sps{c}", [128, 512], F32)) for c in range(4)]
            pvps = [ctx2.enter_context(nc.psum_tensor(f"pvps{c}", [128, 512], F32)) for c in range(2)]
            trp2 = [ctx2.enter_context(nc.psum_tensor(f"trp2_{p}", [128, 8, 128], BF16)) for p in range(2)]

            s_sb = [ctx2.enter_context(nc.sbuf_tensor(f"s_sb{p}", [128, S_LEN], F32)) for p in range(2)]
            p_bf = [ctx2.enter_context(nc.sbuf_tensor(f"p_bf{p}", [128, S_LEN], BF16)) for p in range(2)]
            pt = ctx2.enter_context(nc.sbuf_tensor("pt", [128, NT, 128], BF16))
            xr = [ctx2.enter_context(nc.sbuf_tensor(f"xr{p}", [128, D], F32)) for p in range(2)]
            t_sb = [ctx2.enter_context(nc.sbuf_tensor(f"t_sb{p}", [128, D], F32)) for p in range(2)]
            y1 = [ctx2.enter_context(nc.sbuf_tensor(f"y1_{p}", [128, D], F32)) for p in range(2)]
            sq = [ctx2.enter_context(nc.sbuf_tensor(f"sq{p}", [128, D], F32)) for p in range(2)]
            q8 = [ctx2.enter_context(nc.sbuf_tensor("q8_0", [128, D], FP8))] * 2
            yqb2 = [ctx2.enter_context(nc.sbuf_tensor("yqb2_0", [128, D], BF16))] * 2
            yqb = [ctx2.enter_context(nc.sbuf_tensor(f"yqb{p}", [128, D], BF16)) for p in range(2)]
            yqt = ctx2.enter_context(nc.sbuf_tensor("yqt", [128, ND, 128], BF16))
            t2 = [ctx2.enter_context(nc.sbuf_tensor(f"t2_{p}", [128, D], F32)) for p in range(2)]
            y2 = [ctx2.enter_context(nc.sbuf_tensor(f"y2_{p}", [128, D], F32)) for p in range(2)]



            dg_xr = [0] * NT
            ve_pvevac = [0] * NT
            ac_scopy = [0] * NT
            ve_diag = [0] * NT
            ve_max = [0] * NT
            ac_exp = [0] * NT
            ve_tevac = [0] * NT
            pe_s = [0] * NT
            pe_pv = [0] * NT
            pe_yt = [0] * NT
            pe_ffn = [0] * NT
            ve_sevac_done = [0] * NT
            ve_ytevac = [0] * NT
            ve_ffnevac = [0] * NT
            ve_y2 = [0] * NT
            ve_q = [0] * NT
            ac_rstd1 = [0] * NT
            ac_rstd2 = [0] * NT
            do_store = [0] * NT
            ve_nm1 = [0] * NT
            ve_pvstt_c = [[0, 0] for _ in range(NT)]
            pe_ffn_c = [[0, 0] for _ in range(NT)]
            ve_scopyd = [0] * NT
            tev_groups = [None] * NT
            pe_s_chunk = [None] * NT
            ac_scopy_c = [[0] * 4 for _ in range(NT)]
            pe_yt_a = [0] * NT
            ve_nm2 = [0] * NT
            ac_exp_a = [0] * NT
            exp_split = [False] * NT
            ve_ytev_a = [0] * NT
            ve_y1 = [0] * NT
            yt_tb = [0] * NT
            trp2_free2 = [("ac", 0), ("ac", 0)]
            tb_ctr = 0

            def emit_xr(i):
                nonlocal dg_xr
                k = f"xr{i % 4}"
                if i >= 2:
                    wait("gpsimd", "vp", ve_pvevac[i - 2])
                op("gpsimd", lambda e, i=i, k=k: e.dma_start(out=xr[i % 2][:, :], in_=x_d[i * 128:(i + 1) * 128, :])
                   .then_inc(sem[k], 16))
                v1 = bump(k, 16)
                wait("gpsimd", k, v1)
                op("gpsimd", lambda e, i=i, k=k: e.dma_start(out=xr[i % 2][:, :], in_=x_d[i * 128:(i + 1) * 128, :],
                                                             accum_op=ALU.add).then_inc(sem[k], 16))
                dg_xr[i] = bump(k, 16)

            def emit_S(i):
                par = i % 2
                nch = (i + 1) * 128 // 512 + (1 if ((i + 1) * 128) % 512 else 0)
                if i == 0:
                    wait("tensor", "ve", ve_phase1_done)
                else:
                    wait("tensor", "ac", ac_scopy[i - 1])
                pe_s_chunk[i] = [0] * nch
                for c in range(nch):
                    w = min(512, (i + 1) * 128 - c * 512)
                    for j in range(ND):
                        def f(e, i=i, j=j, c=c, w=w, last=(j == ND - 1)):
                            ins = e.matmul(sps[c][:, 0:w],
                                           ct[:, j, i * 128:(i + 1) * 128],
                                           xfm[:, j, c * 512:c * 512 + w],
                                           start=(j == 0), stop=(j == ND - 1))
                            if last:
                                ins.then_inc(sem["pe"], 1)
                        op("tensor", f)
                    pe_s_chunk[i][c] = bump("pe")
                pe_s[i] = pe_s_chunk[i][-1]

            def emit_scopy(i):
                par = i % 2
                W = (i + 1) * 128
                nch = W // 512 + (1 if W % 512 else 0)
                if i >= 2:
                    wait("scalar", "vs", ve_max[i - 2])
                for c in range(nch):
                    w = min(512, W - c * 512)
                    wait("scalar", "pe", pe_s_chunk[i][c])
                    def f(e, c=c, w=w, par=par):
                        e.copy(s_sb[par][:, c * 512:c * 512 + w], sps[c][:, 0:w]).then_inc(sem["ac"], 1)
                    op("scalar", f)
                    ac_scopy_c[i][c] = bump("ac")
                ac_scopy[i] = ac_scopy_c[i][nch - 1]
                ve_scopyd[i] = 0

            def emit_diag(i):
                par = i % 2
                W = (i + 1) * 128
                nch = W // 512 + (1 if W % 512 else 0)
                dcol = i * 128
                if i == 0:
                    wait("vector", "ds", ds_all)
                if i >= 2:
                    wait("vector", "ac", ac_exp[i - 2])
                for c in range(nch - 1):
                    wait("vector", "ac", ac_scopy_c[i][c])
                    def f(e, c=c, par=par):
                        e.reduce_max(pmax[:, c:c + 1], s_sb[par][:, c * 512:(c + 1) * 512],
                                     axis=AXX, negate=True)
                    op("vector", f)
                wait("vector", "ac", ac_scopy_c[i][nch - 1])
                op("vector", lambda e: e.drain())
                op("vector", lambda e, par=par, dcol=dcol: e.tensor_add(
                    s_sb[par][:, dcol:dcol + 128], s_sb[par][:, dcol:dcol + 128], maskd[:])
                    .then_inc(sem["vs"], 1))
                ve_diag[i] = bump("vs")

            def emit_max(i):
                par = i % 2
                W = (i + 1) * 128
                nch = W // 512 + (1 if W % 512 else 0)
                w = W - (nch - 1) * 512
                op("vector", lambda e: e.drain())
                def f(e, c=nch - 1, w=w, par=par):
                    e.reduce_max(pmax[:, c:c + 1], s_sb[par][:, c * 512:c * 512 + w],
                                 axis=AXX, negate=True)
                op("vector", f)
                op("vector", lambda e: e.drain())
                op("vector", lambda e, par=par, nch=nch: e.tensor_reduce(
                    stat(NMAX, par), pmax[:, 0:nch], AXX, ALU.min).then_inc(sem["vs"], 1))
                ve_max[i] = bump("vs")
            def emit_exp(i):
                par = i % 2
                W = (i + 1) * 128
                Wa = min(W, 1024)
                wait("scalar", "vs", ve_max[i])
                if i >= 2:
                    wait("scalar", "pe", pe_t_last[i - 2])
                op("scalar", lambda e, par=par, Wa=Wa: e.activation(
                    p_bf[par][:, 0:Wa], s_sb[par][:, 0:Wa], AF.Exp,
                    bias=stat(NMAX, par), accum_out=stat(LSUM, par)).then_inc(sem["ac"], 1))
                ac_exp_a[i] = bump("ac")
                if W > 1024:
                    op("scalar", lambda e: e.drain())
                    op("scalar", lambda e, par=par, W=W: e.activation(
                        p_bf[par][:, 1024:W], s_sb[par][:, 1024:W], AF.Exp,
                        bias=stat(NMAX, par), accum_out=stat(LSUMB, par)).then_inc(sem["ac"], 1))
                    ac_exp[i] = bump("ac")
                else:
                    ac_exp[i] = ac_exp_a[i]
                exp_split[i] = (W > 1024)

            def emit_recip(i):
                par = i % 2
                wait("vector", "ac", ac_exp[i])
                if exp_split[i]:
                    op("vector", lambda e, par=par: e.tensor_add(stat(LSUM, par), stat(LSUM, par),
                                                                 stat(LSUMB, par)))
                    op("vector", lambda e: e.drain())
                op("vector", lambda e, par=par: e.reciprocal(stat(RECI, par), stat(LSUM, par))
                   .then_inc(sem["vs"], 1))
                op("vector", lambda e: e.drain())
                bump("vs")

            pe_t_last = [0] * NT

            def emit_T(i):
                nonlocal tb_ctr
                ntile = i + 1
                par = i % 2
                col = 0
                evs = []
                while col < ntile:
                    ng = min(4, ntile - col)
                    tb = tb_ctr % 2
                    tb_ctr += 1
                    wait("tensor", trp2_free2[tb][0], trp2_free2[tb][1])
                    if col < 8:
                        wait("tensor", "ac", ac_exp_a[i])
                    else:
                        wait("tensor", "ac", ac_exp[i])
                    for u in range(ng):
                        def f(e, par=par, col=col, u=u, tb=tb, last=(u == ng - 1)):
                            ins = e.transpose(trp2[tb][:, u, :],
                                              p_bf[par][:, (col + u) * 128:(col + u + 1) * 128], idn[:])
                            if last:
                                ins.then_inc(sem["pe"], 1)
                        op("tensor", f)
                    pe_g = bump("pe")
                    wait("scalar", "pe", pe_g)
                    op("scalar", lambda e, col=col, ng=ng, tb=tb: e.copy(
                        pt[:, col:col + ng, :], trp2[tb][:, 0:ng, :]).then_inc(sem["ac"], 1))
                    trp2_free2[tb] = ("ac", bump("ac"))
                    evs.append(trp2_free2[tb][1])
                    col += ng
                pe_t_last[i] = pe_g
                tev_groups[i] = list(evs)
                ve_tevac[i] = evs[-1]

            pv_free_ve = [("vp", 0)]

            def emit_PV(i):
                par = i % 2
                wait("tensor", pv_free_ve[0][0], pv_free_ve[0][1])
                for t in range(i + 1):
                    if t % 4 == 0:
                        wait("tensor", "ac", tev_groups[i][t // 4])
                    for cc in range(2):
                        def f(e, t=t, cc=cc, i=i, last=(t == i and cc == 1)):
                            ins = e.matmul(pvps[cc][:, 0:512], pt[:, t, :],
                                           vtm[:, t, cc * 512:(cc + 1) * 512],
                                           start=(t == 0), stop=(t == i))
                            if last:
                                ins.then_inc(sem["pe"], 1)
                        op("tensor", f)
                pe_pv[i] = bump("pe")

            def emit_pvstt(i):
                par = i % 2
                wait("vector", "pe", pe_pv[i])
                wait("vector", f"xr{i % 4}", dg_xr[i])
                for cc in range(2):
                    def f(e, cc=cc, par=par):
                        e.scalar_tensor_tensor(t_sb[par][:, cc * 512:(cc + 1) * 512],
                                               pvps[cc][:, 0:512], stat(RECI, par),
                                               xr[par][:, cc * 512:(cc + 1) * 512],
                                               ALU.mult, ALU.add,
                                               accum_out=stat(ST1 if cc == 0 else ST1B, par)
                                               ).then_inc(sem["vp"], 1)
                    op("vector", f)
                    ve_pvstt_c[i][cc] = bump("vp")
                ve_pvevac[i] = ve_pvstt_c[i][1]
                pv_free_ve[0] = ("vp", ve_pvevac[i])
                op("vector", lambda e: e.drain())
                op("vector", lambda e, par=par: e.tensor_scalar(stat(NM1, par), stat(ST1, par),
                                                                stat(ST1B, par), -1.0 / 1024.0,
                                                                ALU.add, ALU.mult).then_inc(sem["vp"], 1))
                ve_nm1[i] = bump("vp")

            def emit_ln1(i):
                par = i % 2
                wait("scalar", "vp", ve_nm1[i])
                op("scalar", lambda e, par=par: e.activation(sq[par][:, :], t_sb[par][:, :], AF.Square,
                                                             bias=stat(NM1, par), accum_out=stat(VS1, par)))
                op("scalar", lambda e: e.drain())
                op("scalar", lambda e, par=par: e.activation(stat(U1, par), stat(VS1, par), AF.Ln,
                                                             scale=cst[:, 1:2], bias=cst[:, 0:1]))
                op("scalar", lambda e: e.drain())
                op("scalar", lambda e, par=par: e.activation(stat(RS1, par), stat(U1, par), AF.Exp,
                                                             scale=cst[:, 2:3]).then_inc(sem["ac"], 1))
                ac_rstd1[i] = bump("ac")

            def emit_y1q(i):
                par = i % 2
                wait("vector", "ac", ac_rstd1[i])
                op("vector", lambda e, par=par: e.tensor_scalar(y1[par][:, :], t_sb[par][:, :],
                                                                stat(NM1, par), stat(RS1, par),
                                                                ALU.add, ALU.mult).then_inc(sem["vp"], 1))
                ve_y1[i] = bump("vp")
                op("vector", lambda e: e.drain())
                op("vector", lambda e, par=par: e.tensor_scalar(sq[par][:, :], y1[par][:, :],
                                                                invh[:, 0:1], 224.0, ALU.mult, ALU.min))
                op("vector", lambda e: e.drain())
                op("vector", lambda e, par=par: e.tensor_scalar(q8[par][:, :], sq[par][:, :],
                                                                -224.0, None, ALU.max))
                op("vector", lambda e: e.drain())
                if i >= 2:
                    wait("vector", "qs0", 16 * (i - 1))
                op("vector", lambda e, par=par: e.tensor_scalar(yqb[par][:, :], q8[par][:, :],
                                                                2.0, None, ALU.mult).then_inc(sem["vp"], 1))
                ve_q[i] = bump("vp")
                if i < NT - 1:
                    wait("sync", "vp", ve_q[i])
                    if i >= 1:
                        wait("sync", "pe", pe_yt[i - 1])
                    op("sync", lambda e, par=par: e.dma_start(
                        out=yqb2[par][:, :].bitcast(mybir.dt.int16),
                        in_=yqb[par][:, :].bitcast(mybir.dt.int16)).then_inc(sem["qs0"], 16))
                    bump("qs0", 16)

            def emit_yT(i):
                nonlocal tb_ctr
                par = i % 2
                tb1 = tb_ctr % 2
                tb_ctr += 1
                tb2 = tb_ctr % 2
                tb_ctr += 1
                yt_tb[i] = (tb1, tb2)
                src_q = yqb if i == NT - 1 else yqb2
                if i == NT - 1:
                    wait("tensor", "vp", ve_q[i])
                else:
                    wait("tensor", "qs0", 16 * (i + 1))
                wait("tensor", trp2_free2[tb1][0], trp2_free2[tb1][1])
                for u in range(4):
                    def f(e, par=par, u=u, tb=tb1, src_q=src_q, last=(u == 3)):
                        ins = e.transpose(trp2[tb][:, u, :], src_q[par][:, u * 128:(u + 1) * 128], idn[:])
                        if last:
                            ins.then_inc(sem["pe"], 1)
                    op("tensor", f)
                pe_yt_a[i] = bump("pe")
                wait("tensor", trp2_free2[tb2][0], trp2_free2[tb2][1])
                for u in range(4, 8):
                    def f(e, par=par, u=u, tb=tb2, src_q=src_q, last=(u == 7)):
                        ins = e.transpose(trp2[tb][:, u - 4, :], src_q[par][:, u * 128:(u + 1) * 128], idn[:])
                        if last:
                            ins.then_inc(sem["pe"], 1)
                    op("tensor", f)
                pe_yt[i] = bump("pe")

            def emit_ytevac(i):
                tb1, tb2 = yt_tb[i]
                wait("scalar", "pe", pe_yt_a[i])
                op("scalar", lambda e, tb=tb1: e.copy(yqt[:, 0:4, :], trp2[tb][:, 0:4, :])
                   .then_inc(sem["ac"], 1))
                trp2_free2[tb1] = ("ac", bump("ac"))
                ve_ytev_a[i] = trp2_free2[tb1][1]
                wait("scalar", "pe", pe_yt[i])
                op("scalar", lambda e, tb=tb2: e.copy(yqt[:, 4:8, :], trp2[tb][:, 0:4, :])
                   .then_inc(sem["ac"], 1))
                trp2_free2[tb2] = ("ac", bump("ac"))
                ve_ytevac[i] = trp2_free2[tb2][1]

            def emit_FFN(i):
                wait("tensor", "ac", ve_ytev_a[i])
                pe_ffn_c[i] = [0, 0]
                for cc in range(2):
                    wait("tensor", "vp", ve_pvstt_c[min(i + 1, NT - 1)][cc])
                    if cc == 0:
                        pass
                    else:
                        wait("tensor", "ac", ve_ytevac[i])
                    for ic in range(ND):
                        if cc == 0 and ic == 4:
                            wait("tensor", "ac", ve_ytevac[i])
                        def f(e, ic=ic, cc=cc, last=(ic == ND - 1)):
                            ins = e.matmul(pvps[cc][:, 0:512], yqt[:, ic, :],
                                           wft[:, ic, cc * 512:(cc + 1) * 512],
                                           start=(ic == 0), stop=(ic == ND - 1))
                            if last:
                                ins.then_inc(sem["pe"], 1)
                        op("tensor", f)
                    pe_ffn_c[i][cc] = bump("pe")
                pe_ffn[i] = pe_ffn_c[i][1]

            def emit_ffnstt(i):
                par = i % 2
                op("vector", lambda e, par=par: e.tensor_add(sq[par][:, :], y1[par][:, :], biasb[:, :]))
                op("vector", lambda e: e.drain())
                if i >= 2:
                    wait("vector", f"do{i % 2}", do_store[i - 2])
                for cc in range(2):
                    wait("vector", "pe", pe_ffn_c[i][cc])
                    def f(e, cc=cc, par=par, last=(cc == 1)):
                        ins = e.scalar_tensor_tensor(t2[par][:, cc * 512:(cc + 1) * 512],
                                                     pvps[cc][:, 0:512], ssb[:, 0:1],
                                                     sq[par][:, cc * 512:(cc + 1) * 512],
                                                     ALU.mult, ALU.add,
                                                     accum_out=stat(ST2 if cc == 0 else ST2B, par))
                        if last:
                            ins.then_inc(sem["vf"], 1)
                    op("vector", f)
                ve_ffnevac[i] = bump("vf")
                pv_free_ve[0] = ("vf", ve_ffnevac[i])
                op("vector", lambda e: e.drain())
                op("vector", lambda e, par=par: e.tensor_scalar(stat(NM2, par), stat(ST2, par),
                                                                stat(ST2B, par), -1.0 / 1024.0,
                                                                ALU.add, ALU.mult).then_inc(sem["vf"], 1))
                ve_nm2[i] = bump("vf")

            def emit_ln2y2(i):
                par = i % 2
                wait("scalar", "vf", ve_nm2[i])
                op("scalar", lambda e, par=par: e.activation(sq[par][:, :], t2[par][:, :], AF.Square,
                                                             bias=stat(NM2, par), accum_out=stat(VS2, par)))
                op("scalar", lambda e: e.drain())
                op("scalar", lambda e, par=par: e.activation(stat(U1, par), stat(VS2, par), AF.Ln,
                                                             scale=cst[:, 1:2], bias=cst[:, 0:1]))
                op("scalar", lambda e: e.drain())
                op("scalar", lambda e, par=par: e.activation(stat(RS2, par), stat(U1, par), AF.Exp,
                                                             scale=cst[:, 2:3]).then_inc(sem["ac"], 1))
                ac_rstd2[i] = bump("ac")
                wait("vector", "ac", ac_rstd2[i])
                if i == NT - 1:
                    for hh in range(2):
                        op("vector", lambda e, par=par, hh=hh: e.tensor_scalar(
                            y2[par][:, hh * 512:(hh + 1) * 512], t2[par][:, hh * 512:(hh + 1) * 512],
                            stat(NM2, par), stat(RS2, par),
                            ALU.add, ALU.mult).then_inc(sem["vf"], 1))
                        vh = bump("vf")
                        wait("sync", "vf", vh)
                        op("sync", lambda e, i=i, par=par, hh=hh: e.dma_start(
                            out=y_d[i * 128:(i + 1) * 128, hh * 512:(hh + 1) * 512],
                            in_=y2[par][:, hh * 512:(hh + 1) * 512]).then_inc(sem[f"do{par}"], 16))
                        do_store[i] = bump(f"do{par}", 16)
                    ve_y2[i] = cnt["vf"]
                else:
                    op("vector", lambda e, par=par: e.tensor_scalar(y2[par][:, :], t2[par][:, :],
                                                                    stat(NM2, par), stat(RS2, par),
                                                                    ALU.add, ALU.mult).then_inc(sem["vf"], 1))
                    ve_y2[i] = bump("vf")
                    wait("sync", "vf", ve_y2[i])
                    op("sync", lambda e, i=i, par=par: e.dma_start(out=y_d[i * 128:(i + 1) * 128, :], in_=y2[par][:, :])
                       .then_inc(sem[f"do{par}"], 16))
                    do_store[i] = bump(f"do{par}", 16)

            if do_phase2 and n_win > 0:
                wait("gpsimd", "pe", pe_phase1_done)
                wait("gpsimd", "ve", ve_phase1_done)
                emit_xr(0)
                if n_win > 1:
                    emit_xr(1)
                emit_S(0)
                emit_scopy(0)
                emit_diag(0)
                emit_max(0)
                emit_exp(0)
                emit_recip(0)
                emit_T(0)
                for i in range(n_win):
                    if i + 1 < n_win:
                        emit_S(i + 1)
                        emit_scopy(i + 1)
                        emit_diag(i + 1)
                        emit_max(i + 1)
                        emit_exp(i + 1)
                    emit_PV(i)
                    emit_pvstt(i)
                    if i >= 1:
                        emit_yT(i - 1)
                        emit_ytevac(i - 1)
                    emit_ln1(i)
                    if i + 2 < n_win:
                        emit_xr(i + 2)
                    if i >= 1:
                        emit_FFN(i - 1)
                        emit_ffnstt(i - 1)
                    emit_y1q(i)
                    if i + 1 < n_win:
                        emit_recip(i + 1)
                        emit_T(i + 1)
                    if i >= 1:
                        emit_ln2y2(i - 1)
                emit_yT(n_win - 1)
                emit_ytevac(n_win - 1)
                emit_FFN(n_win - 1)
                emit_ffnstt(n_win - 1)
                emit_ln2y2(n_win - 1)
                wait("sync", "do0", cnt["do0"])
                wait("sync", "do1", cnt["do1"])
            else:
                wait("vector", "ve", ve_phase1_done)
                op("vector", lambda e: e.tensor_copy(t2[0][:, :], ct[:, 0, 0:1024]).then_inc(sem["ve"], 1))
                vend = bump("ve")
                wait("sync", "ve", vend)
                op("sync", lambda e: e.dma_start(out=y_d[0:128, :], in_=t2[0][:, :]).then_inc(sem["do0"], 16))
                bump("do0", 16)
                wait("sync", "do0", cnt["do0"])

        block = ctx.enter_context(nc.Block())

        def runner(name):
            def run(eng):
                for e, fn in sched:
                    if e == name:
                        fn(eng)
            return run
        block.sync(runner("sync"))
        block.gpsimd(runner("gpsimd"))
        block.tensor(runner("tensor"))
        block.vector(runner("vector"))
        block.scalar(runner("scalar"))

    return nc


_NC_CACHE = [None]


def kernel(L0_attn_x, L0_attn_W, L0_ffn_weight_fp8, L0_ffn_bias, L0_ffn_input_scale, L0_ffn_weight_scale):
    x = np.ascontiguousarray(np.asarray(L0_attn_x, dtype=np.float32))
    W = np.ascontiguousarray(np.asarray(L0_attn_W, dtype=np.float32))
    Wf = np.ascontiguousarray(np.asarray(L0_ffn_weight_fp8, dtype=np.float32))
    bias = np.ascontiguousarray(np.asarray(L0_ffn_bias, dtype=np.float32).reshape(1, D))
    scl = np.ascontiguousarray(np.stack([np.asarray(L0_ffn_input_scale, dtype=np.float32).reshape(()),
                                         np.asarray(L0_ffn_weight_scale, dtype=np.float32).reshape(())]).reshape(1, 2))
    r = np.arange(128)
    mask = np.where(r[None, :] <= r[:, None], np.float32(0.0), np.float32(MASK_NEG)).astype(np.float32)
    idn = np.eye(128, dtype=ml_dtypes.bfloat16)

    if _NC_CACHE[0] is None:
        _NC_CACHE[0] = build_nc()
    nc = _NC_CACHE[0]
    in_maps = [{"x": x[b], "w": W, "wf": Wf, "bias": bias, "scl": scl, "mask": mask, "idn": idn}
               for b in range(8)]
    res = run_bass_kernel_spmd(nc, in_maps, core_ids=list(range(8)))
    return np.stack([res.results[b]["y"] for b in range(8)], axis=0)


